# version 12
# baseline (speedup 1.0000x reference)
import contextlib
import numpy as np
import concourse.bass as bass
import concourse.mybir as mybir

F32 = mybir.dt.float32
BF16 = mybir.dt.bfloat16
I32 = mybir.dt.int32
U16 = mybir.dt.uint16
U8 = mybir.dt.uint8
Alu = mybir.AluOpType
Act = mybir.ActivationFunctionType

D = 4096
TC = 2048
TO = 1024
SC = float(1.0 / np.float64(np.float32(np.sqrt(np.float32(128.0)))))
EPS = 1e-5
HN = 1024

class Ctr:
    def __init__(self, es, nc, name):
        self.sem = es.enter_context(nc.semaphore(name))
        self.n = 0
    def inc(self, instr, k=1):
        self.n += k
        instr.then_inc(self.sem, k)
        return instr


def build(stages, dbg=(), fake_ag=False):
    nc = bass.Bass("TRN2", target_bir_lowering=False, debug=False)
    es = contextlib.ExitStack()
    dram = {}

    def din(name, shape, dt):
        dram[name] = nc.dram_tensor(name, shape, dt, kind="ExternalInput").ap(); return dram[name]
    def dout(name, shape, dt):
        dram[name] = nc.dram_tensor(name, shape, dt, kind="ExternalOutput").ap(); return dram[name]
    def dint(name, shape, dt):
        dram[name] = nc.dram_tensor(name, shape, dt, kind="Internal").ap(); return dram[name]
    def sb(name, shape, dt):
        return es.enter_context(nc.sbuf_tensor(name, shape, dt))
    def psum(name, shape):
        return es.enter_context(nc.psum_tensor(name, shape, F32))

    XT = din("XT", [D, TC], F32)
    XTO = din("XTO", [D, TO], F32)
    WQ = din("WQ", [D, D], F32)
    WK = din("WK", [D, 1024], F32)
    WV = din("WV", [D, 1024], F32)
    WO = din("WO", [D, D], F32)
    QWS = din("QWS", [8, 128, 4, 128], I32)
    LUT4S = din("LUT4S", [128, 4, 16], F32)
    LUT8S = din("LUT8S", [128, 4, 256], F32)
    BIASR = din("BIASR", [128, 32], F32)
    COSO = din("COSO", [128, TO], F32)
    SINO = din("SINO", [128, TO], F32)
    COSC = din("COSC", [128, TC], F32)
    SINC = din("SINC", [128, TC], F32)
    MASKS = din("MASKS", [2, 128, 128], F32)
    OUT = dout("OUT", [D, TO], F32)

    XTb = dint("XTb", [D, TC], BF16)
    XTOb = dint("XTOb", [D, TO], BF16)
    QT = dint("QTs", [D, TO], BF16)
    KT = dint("KTs", [1024, TC], BF16)
    VS = dint("VSs", [TC, 1024], BF16)
    OBT = dint("OBT", [D, TO], BF16)
    R1 = dint("R1", [D, TO], F32)
    X1F = dint("X1F", [D, TO], F32)
    X1B = dint("X1B", [D, TO], BF16)
    R2 = dint("R2", [D, TO], F32)
    AGI = dint("AGI", [512, 8192], BF16)
    AGO = dint("AGO", [D, 8192], BF16)

    dbg_map = {}
    for name in dbg:
        src = dram[name]
        dbg_map[name] = dout("DBG_" + name, list(src.shape), src.dtype)

    PP = [psum(f"pp{i}", [128, 512]) for i in range(6)]
    PS0 = psum("ps0", [1, 512])
    PS1 = psum("ps1", [1, 512])

    wf = [sb(f"wf{i}", [128, 512], F32) for i in range(2)]
    wb = [sb(f"wb{i}", [128, 512], BF16) for i in range(2)]
    xb = [sb(f"xb{i}", [128, TC], BF16) for i in range(2)]
    qraw = sb("qraw", [128, TC], F32)
    qswp = sb("qswp", [128, TC], F32)
    bst = sb("bst", [128, TC], BF16)
    cosO = sb("cosO", [128, TO], F32)
    sinO = sb("sinO", [128, TO], F32)
    cosC = sb("cosC", [128, TC], F32)
    sinC = sb("sinC", [128, TC], F32)
    maskt = sb("maskt", [128, 256], F32)
    biask = sb("biask", [128, 32], F32)

    qh = sb("qh", [128, TO], BF16)
    kh = sb("kh", [128, TC], BF16)
    vh = sb("vh", [128, TC], BF16)
    at_ = [sb(f"at{i}", [128, 512], BF16) for i in range(2)]
    sqs = qswp
    onesb = sb("onesb", [128, 1], BF16)
    onesr1 = sb("onesr1", [1, 128], BF16)
    onesr1F = sb("onesr1F", [1, 128], F32)
    rowsA = sb("rowsA", [1, 5376], F32)
    rowa = rowsA[0:1, 0:TC]
    rowb = rowsA[0:1, TC:TC + 1280]
    negMf = rowsA[0:1, 3328:4352]
    negMb = sb("negMb", [1, TO], BF16)
    rrow = sb("rrow", [1, 512], F32)
    obuf = sb("obuf", [128, 512], BF16)
    s1row = sb("s1row", [1, TO], F32)
    s2row = sb("s2row", [1, TO], F32)
    mrow = sb("mrow", [1, TO], F32)
    rstd = sb("rstd", [1, TO], F32)
    mskrow = sb("mskrow", [1, TO], F32)
    erow = rowsA[0:1, 4352:5376]
    m2row = mrow
    r2row = rstd
    maskbc = sb("maskbc", [128, TO], F32)
    rtile = sb("rtile", [128, 512], F32)
    rtile2 = sb("rtile2", [128, 512], F32)
    x1r = [sb(f"x1r{i}", [128, TO], BF16) for i in range(2)]
    onesbF = sb("onesbF", [128, 1], F32)

    es_dq = contextlib.ExitStack()
    def sbq(name, shape, dt):
        return es_dq.enter_context(nc.sbuf_tensor(name, shape, dt))
    dq_qw8 = sbq("dq_qw8", [128, 8 * 512], U8)
    dq_l4 = sbq("dq_l4", [128, 16], F32)
    dq_l8 = sbq("dq_l8", [128, 256], F32)
    dq_idx = sbq("dq_idx", [128, 4096], U16)
    dq_ci = sbq("dq_ci", [128, HN], I32)
    dq_t8 = sbq("dq_t8", [128, 512], U8)
    dq_wl = sbq("dq_wl", [128, HN], BF16)
    dq_wlt = sbq("dq_wlt", [128, HN], BF16)
    dq_wh = sbq("dq_wh", [128, HN], BF16)
    dq_V16 = sbq("dq_V16", [128, 256], I32)
    dq_W32 = sbq("dq_W32", [128, 4 * 32], I32)
    dq_XW = sbq("dq_XW", [128, 4 * 16], I32)
    dq_msk = sbq("dq_msk", [128, 5 * HN], I32)
    dq_mux = sbq("dq_mux", [128, 5 * HN], I32)
    dq_sham = sbq("dq_sham", [128, HN], I32)
    dq_acc = dq_ci
    dq_t32 = sbq("dq_t32", [128, HN], I32)

    cLD = Ctr(es, nc, "cLD")
    sCONST = Ctr(es, nc, "sCONST")
    sXB = [Ctr(es, nc, "sXB0"), Ctr(es, nc, "sXB1")]
    sWF = [Ctr(es, nc, "sWF0"), Ctr(es, nc, "sWF1")]
    sSW = Ctr(es, nc, "sSW")
    sBST = Ctr(es, nc, "sBST")
    sOBUF = Ctr(es, nc, "sOBUF")
    sQH = Ctr(es, nc, "sQH")
    sKH = Ctr(es, nc, "sKH")
    sVH = Ctr(es, nc, "sVH")
    sOB = Ctr(es, nc, "sOB")
    sOL = [Ctr(es, nc, "sOL0"), Ctr(es, nc, "sOL1")]
    sRT = Ctr(es, nc, "sRT")
    sX1 = Ctr(es, nc, "sX1")
    cST = Ctr(es, nc, "cST")
    cGD = Ctr(es, nc, "cGD")
    cMM = Ctr(es, nc, "cMM")
    cSE = Ctr(es, nc, "cSE")
    cDV = Ctr(es, nc, "cDV")
    cAG = Ctr(es, nc, "cAG")
    sDQ = Ctr(es, nc, "sDQ")
    sDQS = Ctr(es, nc, "sDQS")
    sXR = Ctr(es, nc, "sXR")
    sOUT = Ctr(es, nc, "sOUT")
    ALL_CTRS = [cLD, sCONST, sSW, sBST, sOBUF, sQH, sKH, sVH, sOB, sRT, sX1, cST, cGD, cMM, cSE, cDV, cAG, sDQ, sDQS, sXR, sOUT] + sXB + sWF + sOL

    blk = es.enter_context(nc.Block())

    @blk.sync
    def _(s):
        sCONST.inc(s.dma_start(out=cosO[:], in_=COSO[:]), 16)
        sCONST.inc(s.dma_start(out=sinO[:], in_=SINO[:]), 16)
        sCONST.inc(s.dma_start(out=cosC[:], in_=COSC[:]), 16)
        sCONST.inc(s.dma_start(out=sinC[:], in_=SINC[:]), 16)
        sCONST.inc(s.dma_start(out=maskt[:, 0:128], in_=MASKS[0]), 16)
        sCONST.inc(s.dma_start(out=maskt[:, 128:256], in_=MASKS[1]), 16)
        sCONST.inc(s.dma_start(out=biask[:], in_=BIASR[:]), 16)

    @blk.vector
    def _(v):
        cDV.inc(v.memset(onesb[:], 1.0))
        cDV.inc(v.memset(onesr1[:], 1.0))
        cDV.inc(v.memset(onesr1F[:], 1.0))
        cDV.inc(v.memset(onesbF[:], 1.0))
        v.wait_ge(sCONST.sem, sCONST.n)

    if "p1" in stages:
        @blk.gpsimd
        def _(g):
            for i in range(8):
                cGD.inc(g.dma_start(out=XTb[i * 512:(i + 1) * 512, :], in_=XT[i * 512:(i + 1) * 512, :]), 16)
            for i in range(4):
                cGD.inc(g.dma_start(out=XTOb[i * 1024:(i + 1) * 1024, :], in_=XTO[i * 1024:(i + 1) * 1024, :]), 16)
    n_cast = cGD.n

    guard = {"mm": [0, 0], "wcast": [0, 0]}

    def proj_std(MCH, TN, W, XSRC, OUTT, cost, sint, G):
        ntt = TN // 512
        i0 = 0
        while i0 < MCH:
            grp = list(range(i0, min(i0 + G, MCH)))
            i0 += G
            gw = len(grp)
            se_gate = cSE.n
            for ic in range(32):
                pb = ic % 2
                @blk.sync
                def _(s, ic=ic, pb=pb, grp=grp, gw=gw, mg=guard["mm"][pb], W=W, XSRC=XSRC, TN=TN):
                    s.wait_ge(cMM.sem, mg)
                    sXB[pb].inc(s.dma_start(out=xb[pb][:, 0:TN], in_=XSRC[ic * 128:(ic + 1) * 128, :]), 16)
                    sWF[pb].inc(s.dma_start(out=wf[pb][:, 0:gw * 128],
                                        in_=W[ic * 128:(ic + 1) * 128, grp[0] * 128:(grp[0] + gw) * 128]), 16)
                ldw_n = sWF[pb].n
                ldx_n = sXB[pb].n
                @blk.vector
                def _(v, pb=pb, gw=gw, need=ldw_n, wg=guard["wcast"][pb]):
                    v.wait_ge(sWF[pb].sem, need)
                    v.wait_ge(cMM.sem, wg)
                    cDV.inc(v.tensor_copy(out=wb[pb][:, 0:gw * 128], in_=wf[pb][:, 0:gw * 128]))
                dvn = cDV.n
                @blk.tensor
                def _(t, ic=ic, pb=pb, gw=gw, ntt=ntt, need=dvn, needx=ldx_n, seg=se_gate):
                    t.wait_ge(cDV.sem, need)
                    t.wait_ge(sXB[pb].sem, needx)
                    if ic == 0:
                        t.wait_ge(cSE.sem, seg)
                    for mi in range(gw):
                        for tt in range(ntt):
                            cMM.inc(t.matmul(PP[mi * ntt + tt][:],
                                             wb[pb][:, mi * 128:(mi + 1) * 128],
                                             xb[pb][:, tt * 512:(tt + 1) * 512],
                                             start=(ic == 0), stop=(ic == 31)))
                guard["mm"][pb] = cMM.n
                guard["wcast"][pb] = cMM.n
            mm_done = cMM.n
            for mi, m in enumerate(grp):
                for tt in range(ntt):
                    @blk.scalar
                    def _(sc, pt=PP[mi * ntt + tt], toff=tt * 512, need=mm_done, dn=cDV.n):
                        sc.wait_ge(cMM.sem, need)
                        sc.wait_ge(cDV.sem, dn)
                        cSE.inc(sc.copy(out=qraw[:, toff:toff + 512], in_=pt[:]))
                sen = cSE.n
                @blk.sync
                def _(s, TN=TN, need=sen, dvp=cDV.n):
                    s.wait_ge(cSE.sem, need)
                    s.wait_ge(cDV.sem, dvp)
                    sSW.inc(s.dma_start(out=qswp[0:64, 0:TN], in_=qraw[64:128, 0:TN]), 16)
                    sSW.inc(s.dma_start(out=qswp[64:128, 0:TN], in_=qraw[0:64, 0:TN]), 16)
                ldn2 = sSW.n
                @blk.vector
                def _(v, TN=TN, cost=cost, sint=sint, needL=ldn2, needS=sen):
                    v.wait_ge(sSW.sem, needL)
                    v.wait_ge(cSE.sem, needS)
                    cDV.inc(v.tensor_mul(out=qraw[:, 0:TN], in0=qraw[:, 0:TN], in1=cost[:, 0:TN]))
                    cDV.inc(v.tensor_mul(out=qswp[:, 0:TN], in0=qswp[:, 0:TN], in1=sint[:, 0:TN]))
                    v.wait_ge(cDV.sem, cDV.n)
                    cDV.inc(v.tensor_add(out=qraw[:, 0:TN], in0=qraw[:, 0:TN], in1=qswp[:, 0:TN]))
                    v.wait_ge(cDV.sem, cDV.n)
                    cDV.inc(v.tensor_copy(out=bst[:, 0:TN], in_=qraw[:, 0:TN]))
                dvn3 = cDV.n
                @blk.scalar
                def _(sc, m=m, TN=TN, OUTT=OUTT, need=dvn3):
                    sc.wait_ge(cDV.sem, need)
                    sBST.inc(sc.dma_start(out=OUTT[m * 128:(m + 1) * 128, 0:TN], in_=bst[:, 0:TN]), 16)
                stn = sBST.n
                @blk.vector
                def _(v, need=stn):
                    v.wait_ge(sBST.sem, need)

    def proj_v():
        i0 = 0
        while i0 < 16:
            grp = list(range(i0, min(i0 + 3, 16)))
            i0 += 3
            gw = len(grp)
            se_gate = cSE.n
            for half in range(2):
                for ic in range(32):
                    pb = ic % 2
                    @blk.sync
                    def _(s, ic=ic, pb=pb, half=half, mg=guard["mm"][pb]):
                        s.wait_ge(cMM.sem, mg)
                        sXB[pb].inc(s.dma_start(out=xb[pb][:, 0:TC], in_=XTb[ic * 128:(ic + 1) * 128, :]), 16)
                        sWF[pb].inc(s.dma_start(out=wf[pb][:, 0:512],
                                            in_=WV[ic * 128:(ic + 1) * 128, half * 512:(half + 1) * 512]), 16)
                    ldw_n = sWF[pb].n
                    ldx_n = sXB[pb].n
                    @blk.vector
                    def _(v, pb=pb, need=ldw_n, wg=guard["wcast"][pb]):
                        v.wait_ge(sWF[pb].sem, need)
                        v.wait_ge(cMM.sem, wg)
                        cDV.inc(v.tensor_copy(out=wb[pb][:, 0:512], in_=wf[pb][:, 0:512]))
                    dvn = cDV.n
                    @blk.tensor
                    def _(t, ic=ic, pb=pb, grp=grp, gw=gw, half=half, need=dvn, needx=ldx_n, seg=se_gate):
                        t.wait_ge(cDV.sem, need)
                        t.wait_ge(sXB[pb].sem, needx)
                        if ic == 0:
                            t.wait_ge(cSE.sem, seg)
                        for mi, m in enumerate(grp):
                            cMM.inc(t.matmul(PP[mi * 2 + half][:], xb[pb][:, m * 128:(m + 1) * 128],
                                             wb[pb][:, 0:512], start=(ic == 0), stop=(ic == 31)))
                    guard["mm"][pb] = cMM.n
                    guard["wcast"][pb] = cMM.n
            mm_done = cMM.n
            last_st = [sOBUF.n]
            for mi, m in enumerate(grp):
                for half in range(2):
                    @blk.scalar
                    def _(sc, pt=PP[mi * 2 + half], need=mm_done, lst=last_st[0]):
                        sc.wait_ge(cMM.sem, need)
                        sc.wait_ge(sOBUF.sem, lst)
                        cSE.inc(sc.copy(out=obuf[:], in_=pt[:]))
                    sen = cSE.n
                    @blk.scalar
                    def _(sc, m=m, half=half, need=sen):
                        sc.wait_ge(cSE.sem, need)
                        sOBUF.inc(sc.dma_start(out=VS[m * 128:(m + 1) * 128, half * 512:(half + 1) * 512],
                                             in_=obuf[:]), 16)
                    last_st[0] = sOBUF.n

    if "p1" in stages:
        @blk.sync
        def _(s):
            s.wait_ge(cGD.sem, n_cast)
        proj_std(32, TO, WQ, XTOb, QT, cosO, sinO, 3)
        proj_std(8, TC, WK, XTb, KT, cosC, sinC, 1)
        proj_v()



    FXS = 16
    dq_state = {"steps": [], "i": 0}
    def dq_make_steps():
        steps = dq_state["steps"]
        lut_gate = [0]
        for ch in range(4):
            def load_chunk(ch=ch):
                @blk.gpsimd
                def _(g, ch=ch, dv=cDV.n):
                    g.wait_ge(cDV.sem, dv)
                    for p in range(8):
                        sDQ.inc(g.dma_start(out=dq_qw8[:, p * 512:(p + 1) * 512],
                                            in_=QWS[p, :, ch, :].bitcast(U8)), 16)
                    sDQ.inc(g.dma_start(out=dq_l4[:], in_=LUT4S[:, ch, :]), 16)
                    sDQ.inc(g.dma_start(out=dq_l8[:], in_=LUT8S[:, ch, :]), 16)
            steps.append(load_chunk)
            ldq_after = None
            def ext_init(ch=ch):
                @blk.vector
                def _(v, need=sDQ.n):
                    v.wait_ge(sDQ.sem, need)
                    cDV.inc(v.memset(dq_idx[:], 0))
                    v.wait_ge(cDV.sem, cDV.n)
            steps.append(ext_init)
            for p in range(8):
                def ext_plane(p=p):
                    @blk.vector
                    def _(v, p=p):
                        for r in range(8):
                            m = (1 << r) * 0x01010101
                            if m >= 2 ** 31:
                                m -= 2 ** 32
                            cDV.inc(v.tensor_scalar(out=dq_t8.bitcast(I32)[:, 0:128],
                                                    in0=dq_qw8.bitcast(I32)[:, p * 128:(p + 1) * 128],
                                                    scalar1=m, scalar2=None, op0=Alu.bitwise_and))
                            v.wait_ge(cDV.sem, cDV.n)
                            sl = dq_idx[:].rearrange("o (c r) -> o c r", r=8)[:, :, r]
                            cDV.inc(v.scalar_tensor_tensor(out=sl, in0=dq_t8[:], scalar=float(2.0 ** (p - r)),
                                                           in1=sl, op0=Alu.mult, op1=Alu.add))
                            v.wait_ge(cDV.sem, cDV.n)
                steps.append(ext_plane)
            def tprep(ch=ch):
                @blk.vector
                def _(v):
                    cDV.inc(v.tensor_scalar(out=dq_V16[:], in0=dq_l8[:], scalar1=float(2.0 ** FXS),
                                            scalar2=32768.0, op0=Alu.mult, op1=Alu.add))
                    v.wait_ge(cDV.sem, cDV.n)
                    for sl4 in range(4):
                        base = sl4 * 32
                        cDV.inc(v.tensor_scalar(out=dq_t32[:, 0:256], in0=dq_V16[:], scalar1=4 * sl4,
                                                scalar2=15, op0=Alu.logical_shift_right, op1=Alu.bitwise_and))
                        v.wait_ge(cDV.sem, cDV.n)
                        for k in range(8):
                            sl = dq_t32[:, 0:256].rearrange("o (j k) -> o j k", k=8)[:, :, k]
                            if k == 0:
                                cDV.inc(v.tensor_copy(out=dq_W32[:, base:base + 32], in_=sl))
                            else:
                                cDV.inc(v.tensor_scalar(out=dq_t32[:, 256:288], in0=sl, scalar1=4 * k,
                                                        scalar2=None, op0=Alu.logical_shift_left))
                                v.wait_ge(cDV.sem, cDV.n)
                                cDV.inc(v.tensor_tensor(out=dq_W32[:, base:base + 32],
                                                        in0=dq_W32[:, base:base + 32],
                                                        in1=dq_t32[:, 256:288], op=Alu.bitwise_or))
                            v.wait_ge(cDV.sem, cDV.n)
                    for sl4 in range(4):
                        a = dq_W32[:, sl4 * 32:(sl4 + 1) * 32].rearrange("o (m e) -> o m e", e=2)
                        cDV.inc(v.tensor_tensor(out=dq_XW[:, sl4 * 16:(sl4 + 1) * 16], in0=a[:, :, 0],
                                                in1=a[:, :, 1], op=Alu.bitwise_xor))
                    v.wait_ge(cDV.sem, cDV.n)
            steps.append(tprep)
            for q in range(4):
                def quarter_pre(ch=ch, q=q):
                    @blk.vector
                    def _(v, q=q, dv=None):
                        v.wait_ge(cDV.sem, cDV.n)
                        cDV.inc(v.tensor_copy(out=dq_ci[:], in_=dq_idx[:, q * HN:(q + 1) * HN]))
                        v.wait_ge(cDV.sem, cDV.n)
                        cDV.inc(v.tensor_scalar(out=dq_msk[:, 0:HN], in0=dq_ci[:], scalar1=8, scalar2=None,
                                                op0=Alu.bitwise_and))
                        v.wait_ge(cDV.sem, cDV.n)
                        cDV.inc(v.tensor_scalar(out=dq_msk[:, 0:HN], in0=dq_msk[:, 0:HN], scalar1=-0.125,
                                                scalar2=None, op0=Alu.mult))
                        for k in range(1, 5):
                            cDV.inc(v.tensor_scalar(out=dq_msk[:, k * HN:(k + 1) * HN], in0=dq_ci[:],
                                                    scalar1=(1 << (k + 3)), scalar2=None, op0=Alu.bitwise_and))
                        cDV.inc(v.tensor_scalar(out=dq_sham[:], in0=dq_ci[:], scalar1=7, scalar2=None,
                                                op0=Alu.bitwise_and))
                        v.wait_ge(cDV.sem, cDV.n)
                        cDV.inc(v.tensor_scalar(out=dq_sham[:], in0=dq_sham[:], scalar1=4.0, scalar2=None,
                                                op0=Alu.mult))
                        cDV.inc(v.tensor_scalar(out=dq_t32[:], in0=dq_ci[:], scalar1=15, scalar2=None,
                                                op0=Alu.bitwise_and))
                        v.wait_ge(cDV.sem, cDV.n)
                    @blk.vector
                    def _(v):
                        for c in range(16):
                            cDV.inc(v.tensor_scalar(out=dq_wlt[:], in0=dq_t32[:], scalar1=float(c),
                                                    scalar2=dq_l4[:, c:c + 1], op0=Alu.is_equal, op1=Alu.mult))
                            v.wait_ge(cDV.sem, cDV.n)
                            if c == 0:
                                cDV.inc(v.tensor_copy(out=dq_wl[:], in_=dq_wlt[:]))
                            else:
                                cDV.inc(v.tensor_tensor(out=dq_wl[:], in0=dq_wl[:], in1=dq_wlt[:], op=Alu.add))
                            v.wait_ge(cDV.sem, cDV.n)
                    @blk.gpsimd
                    def _(g, ch=ch, q=q, dv=cDV.n):
                        g.wait_ge(cDV.sem, dv)
                        sDQS.inc(g.dma_start(out=AGI[ch * 128:(ch + 1) * 128, q * HN:(q + 1) * HN],
                                             in_=dq_wl[:]), 16)
                steps.append(quarter_pre)
                def mslot(j):
                    return dq_mux[:, j * HN:(j + 1) * HN]
                for sl4 in range(4):
                    def mux_slice(ch=ch, q=q, sl4=sl4):
                        @blk.vector
                        def _(v, sl4=sl4):
                            filled = [False] * 4
                            for mm_ in range(16):
                                cur = 4
                                cDV.inc(v.tensor_scalar(out=mslot(cur), in0=dq_msk[:, 0:HN],
                                                        scalar1=dq_XW[:, sl4 * 16 + mm_:sl4 * 16 + mm_ + 1],
                                                        scalar2=dq_W32[:, sl4 * 32 + 2 * mm_:sl4 * 32 + 2 * mm_ + 1],
                                                        op0=Alu.bitwise_and, op1=Alu.bitwise_xor))
                                v.wait_ge(cDV.sem, cDV.n)
                                lev = 1
                                while lev <= 4 and filled[lev - 1]:
                                    cDV.inc(v.copy_predicated(out=mslot(lev - 1),
                                                              mask=dq_msk[:, lev * HN:(lev + 1) * HN],
                                                              data=mslot(cur)))
                                    v.wait_ge(cDV.sem, cDV.n)
                                    filled[lev - 1] = False
                                    cur = lev - 1
                                    lev += 1
                                if lev <= 4:
                                    if cur != lev - 1:
                                        cDV.inc(v.tensor_copy(out=mslot(lev - 1), in_=mslot(cur)))
                                        v.wait_ge(cDV.sem, cDV.n)
                                    filled[lev - 1] = True
                            res = mslot(3)
                            if sl4 == 0:
                                cDV.inc(v.tensor_tensor(out=dq_acc[:], in0=res, in1=dq_sham[:],
                                                        op=Alu.logical_shift_right))
                                v.wait_ge(cDV.sem, cDV.n)
                                cDV.inc(v.tensor_scalar(out=dq_acc[:], in0=dq_acc[:], scalar1=15,
                                                        scalar2=None, op0=Alu.bitwise_and))
                            else:
                                cDV.inc(v.tensor_tensor(out=dq_t32[:], in0=res, in1=dq_sham[:],
                                                        op=Alu.logical_shift_right))
                                v.wait_ge(cDV.sem, cDV.n)
                                cDV.inc(v.tensor_scalar(out=dq_t32[:], in0=dq_t32[:], scalar1=15,
                                                        scalar2=None, op0=Alu.bitwise_and))
                                v.wait_ge(cDV.sem, cDV.n)
                                cDV.inc(v.tensor_scalar(out=dq_t32[:], in0=dq_t32[:],
                                                        scalar1=float(2.0 ** (4 * sl4)),
                                                        scalar2=None, op0=Alu.mult))
                                v.wait_ge(cDV.sem, cDV.n)
                                cDV.inc(v.tensor_tensor(out=dq_acc[:], in0=dq_acc[:], in1=dq_t32[:], op=Alu.add))
                            v.wait_ge(cDV.sem, cDV.n)
                    steps.append(mux_slice)
                def wh_fin(ch=ch, q=q):
                    @blk.vector
                    def _(v, stp=sDQS.n):
                        v.wait_ge(sDQS.sem, stp)
                        cDV.inc(v.tensor_scalar(out=dq_wh[:], in0=dq_acc[:], scalar1=32768.0,
                                                scalar2=float(2.0 ** -FXS), op0=Alu.subtract, op1=Alu.mult))
                    dvn = cDV.n
                    @blk.gpsimd
                    def _(g, ch=ch, q=q, need=dvn):
                        g.wait_ge(cDV.sem, need)
                        sDQS.inc(g.dma_start(out=AGI[ch * 128:(ch + 1) * 128, 4096 + q * HN:4096 + (q + 1) * HN],
                                             in_=dq_wh[:]), 16)
                steps.append(wh_fin)
    def dq_pump(k):
        st = dq_state["steps"]
        while k > 0 and dq_state["i"] < len(st):
            st[dq_state["i"]]()
            dq_state["i"] += 1
            k -= 1
    def dq_finish():
        dq_pump(len(dq_state["steps"]))
        if fake_ag:
            @blk.gpsimd
            def _(g, need=sDQS.n):
                g.wait_ge(sDQS.sem, need)
                for c8 in range(8):
                    sDQS.inc(g.dma_start(out=AGO[c8 * 512:(c8 + 1) * 512, :], in_=AGI[:]), 16)
                cAG.inc(g.sem_inc(cAG.sem, 0)) if False else None
            @blk.gpsimd
            def _(g, need=sDQS.n):
                g.wait_ge(sDQS.sem, need)
                cAG.inc(g.engine_nop()) if False else None
                cAG.n += 1
                g.sem_inc(cAG.sem, 1)
        else:
            @blk.gpsimd
            def _(g, need=sDQS.n):
                g.wait_ge(sDQS.sem, need)
                cAG.inc(g.collective_compute(
                    "AllGather", mybir.AluOpType.bypass,
                    replica_groups=[list(range(8))],
                    ins=[AGI[:]],
                    outs=[AGO[:]],
                ))

    def p2_attention():
        sqs_b = bst
        von = sOBUF.n
        bstn = sBST.n
        for kh in range(8):
            if "dq" in stages:
                dq_pump(26)
            @blk.sync
            def _(s, kh=kh, dv=cDV.n, se=cSE.n, mm=cMM.n, von=von, bstn=bstn):
                s.wait_ge(cDV.sem, dv)
                s.wait_ge(cMM.sem, mm)
                s.wait_ge(sOBUF.sem, von)
                s.wait_ge(sBST.sem, bstn)
                sKH.inc(s.dma_start(out=kh_t[:], in_=KT[kh * 128:(kh + 1) * 128, :]), 16)
                for tkc in range(16):
                    sVH.inc(s.dma_start(out=vh[:, tkc * 128:(tkc + 1) * 128],
                                        in_=VS[tkc * 128:(tkc + 1) * 128, kh * 128:(kh + 1) * 128]), 16)
            khn = sKH.n
            vhn = sVH.n
            @blk.scalar
            def _(sc, need=khn):
                sc.wait_ge(sKH.sem, need)
                cSE.inc(sc.activation(out=sqs_b[:, 0:TC], in_=kh_t[:], func=Act.Square))
            sen = cSE.n
            for q4 in range(4):
                @blk.tensor
                def _(t, q4=q4, need=sen, dv=cDV.n):
                    t.wait_ge(cSE.sem, need)
                    t.wait_ge(cDV.sem, dv)
                    cMM.inc(t.matmul(PS0[:], onesb[:], sqs_b[:, q4 * 512:(q4 + 1) * 512],
                                     start=True, stop=True))
                mmn = cMM.n
                @blk.vector
                def _(v, q4=q4, need=mmn):
                    v.wait_ge(cMM.sem, need)
                    cDV.inc(v.tensor_copy(out=rowa[:, q4 * 512:(q4 + 1) * 512], in_=PS0[:]))
            dvn = cDV.n
            @blk.vector
            def _(v, need=dvn):
                v.wait_ge(cDV.sem, need)
                cDV.inc(v.reduce_max(out=rowb[:, 0:1], in_=rowa[:, 0:TC], axis=mybir.AxisListType.X))
            mkn = cDV.n

            for hq in range(4):
                h = kh * 4 + hq
                @blk.sync
                def _(s, h=h, dv=cDV.n, mm=cMM.n):
                    s.wait_ge(cDV.sem, dv)
                    s.wait_ge(cMM.sem, mm)
                    sQH.inc(s.dma_start(out=qh[:], in_=QT[h * 128:(h + 1) * 128, :]), 16)
                qhn = sQH.n
                @blk.scalar
                def _(sc, need=qhn, dv=mkn):
                    sc.wait_ge(sQH.sem, need)
                    sc.wait_ge(cDV.sem, dv)
                    cSE.inc(sc.activation(out=sqs_b[:, 0:TO], in_=qh[:], func=Act.Square))
                sen2 = cSE.n
                for q2 in range(2):
                    @blk.tensor
                    def _(t, q2=q2, need=sen2, dv=cDV.n):
                        t.wait_ge(cSE.sem, need)
                        t.wait_ge(cDV.sem, dv)
                        cMM.inc(t.matmul(PS0[:], onesb[:], sqs_b[:, q2 * 512:(q2 + 1) * 512],
                                         start=True, stop=True))
                    mmn2 = cMM.n
                    @blk.vector
                    def _(v, q2=q2, need=mmn2):
                        v.wait_ge(cMM.sem, need)
                        cDV.inc(v.tensor_copy(out=rowa[:, 1024 + q2 * 512:1024 + (q2 + 1) * 512], in_=PS0[:]))
                dvq = cDV.n
                @blk.vector
                def _(v, need=dvq):
                    v.wait_ge(cDV.sem, need)
                    cDV.inc(v.tensor_scalar(out=rowa[:, 1024:1024 + TO], in0=rowa[:, 1024:1024 + TO],
                                            scalar1=rowb[:, 0:1], scalar2=None, op0=Alu.mult))
                dvq2 = cDV.n
                @blk.scalar
                def _(sc, need=dvq2):
                    sc.wait_ge(cDV.sem, need)
                    cSE.inc(sc.activation(out=negMf[:], in_=rowa[:, 1024:1024 + TO], func=Act.Sqrt))
                seq = cSE.n
                @blk.vector
                def _(v, need=seq):
                    v.wait_ge(cSE.sem, need)
                    cDV.inc(v.tensor_scalar(out=negMb[:], in0=negMf[:], scalar1=-1.01, scalar2=None,
                                            op0=Alu.mult))
                negn = cDV.n

                for tt in range(2):
                    ntk = 8 * tt + 8
                    for tkc in range(ntk):
                        jj = max(0, (tkc >> 1) - 4 * tt)
                        noff = jj * 128
                        nw = 512 - noff
                        sp = PP[tkc % 2]
                        atb = at_[tkc % 2]
                        @blk.tensor
                        def _(t, sp=sp, tkc=tkc, tt=tt, jj=jj, noff=noff, nw=nw, neg=negn,
                              qn=qhn, kn=khn, se=cSE.n, dv=cDV.n):
                            t.wait_ge(cDV.sem, neg)
                            t.wait_ge(sQH.sem, qn)
                            t.wait_ge(sKH.sem, kn)
                            t.wait_ge(cSE.sem, se)
                            t.wait_ge(cDV.sem, dv)
                            cMM.inc(t.matmul(sp[:, noff:512], onesr1[:],
                                             negMb[0:1, tt * 512 + noff:(tt + 1) * 512],
                                             start=True, stop=False))
                            cMM.inc(t.matmul(sp[:, noff:512], kh_t[:, tkc * 128:(tkc + 1) * 128],
                                             qh[:, tt * 512 + noff:(tt + 1) * 512],
                                             start=False, stop=True))
                        mmn3 = cMM.n
                        j = tkc >> 1
                        if 4 * tt <= j < 4 * tt + 4:
                            @blk.vector
                            def _(v, sp=sp, j=j, tt=tt, sel=tkc & 1, need=mmn3):
                                v.wait_ge(cMM.sem, need)
                                cDV.inc(v.tensor_tensor(out=sp[:, (j - 4 * tt) * 128:(j - 4 * tt + 1) * 128],
                                                        in0=sp[:, (j - 4 * tt) * 128:(j - 4 * tt + 1) * 128],
                                                        in1=maskt[:, sel * 128:(sel + 1) * 128], op=Alu.add))
                        dmn = cDV.n
                        if noff > 0:
                            @blk.vector
                            def _(v, atb=atb, noff=noff, se=cSE.n):
                                v.wait_ge(cSE.sem, se)
                                cDV.inc(v.memset(atb[:, 0:noff], 0.0))
                        dmn2 = cDV.n
                        @blk.scalar
                        def _(sc, sp=sp, atb=atb, noff=noff, need=mmn3, dv=dmn, dv2=dmn2, mmprev=cMM.n):
                            sc.wait_ge(cMM.sem, mmprev)
                            sc.wait_ge(cDV.sem, dv2)
                            cSE.inc(sc.activation(out=atb[:, noff:512], in_=sp[:, noff:512],
                                                  func=Act.Exp, scale=SC))
                        expn = cSE.n
                        @blk.tensor
                        def _(t, atb=atb, tkc=tkc, ntk=ntk, need=expn, vn=vhn):
                            t.wait_ge(cSE.sem, need)
                            t.wait_ge(sVH.sem, vn)
                            cMM.inc(t.matmul(PP[2][:], vh[:, tkc * 128:(tkc + 1) * 128], atb[:],
                                             start=(tkc == 0), stop=(tkc == ntk - 1)))
                            cMM.inc(t.matmul(PS1[:], onesb[:], atb[:],
                                             start=(tkc == 0), stop=(tkc == ntk - 1)))
                    avn = cMM.n
                    @blk.scalar
                    def _(sc, need=avn, dv=cDV.n):
                        sc.wait_ge(cMM.sem, need)
                        sc.wait_ge(cDV.sem, dv)
                        cSE.inc(sc.copy(out=rrow[:], in_=PS1[:]))
                    sen3 = cSE.n
                    @blk.vector
                    def _(v, need=sen3):
                        v.wait_ge(cSE.sem, need)
                        cDV.inc(v.reciprocal(out=rrow[:], in_=rrow[:]))
                    rcn = cDV.n
                    @blk.tensor
                    def _(t, need=rcn):
                        t.wait_ge(cDV.sem, need)
                        cMM.inc(t.matmul(PP[4][:], onesr1F[:], rrow[:], start=True, stop=True))
                    rbn = cMM.n
                    @blk.scalar
                    def _(sc, need=rbn, dv=cDV.n):
                        sc.wait_ge(cMM.sem, need)
                        sc.wait_ge(cDV.sem, dv)
                        cSE.inc(sc.copy(out=rtile[:], in_=PP[4][:]))
                    rbe = cSE.n
                    @blk.vector
                    def _(v, need=rbe, ob=sOB.n):
                        v.wait_ge(cSE.sem, need)
                        v.wait_ge(sOB.sem, ob)
                        cDV.inc(v.tensor_mul(out=obuf[:], in0=PP[2][:], in1=rtile[:]))
                    obn = cDV.n
                    @blk.scalar
                    def _(sc, h=h, tt=tt, need=obn):
                        sc.wait_ge(cDV.sem, need)
                        sOB.inc(sc.dma_start(out=OBT[h * 128:(h + 1) * 128, tt * 512:(tt + 1) * 512],
                                             in_=obuf[:]), 16)

    def p2b_wo():
        obn_all = sOB.n
        @blk.sync
        def _(s, need=obn_all):
            s.wait_ge(sOB.sem, need)
        @blk.vector
        def _(v):
            cDV.inc(v.memset(s1row[:], 0.0))
            cDV.inc(v.memset(s2row[:], 0.0))
        i0 = 0
        while i0 < 32:
            grp = list(range(i0, min(i0 + 3, 32)))
            i0 += 3
            gw = len(grp)
            se_gate = cSE.n
            dv_gate = cDV.n
            for dh in range(32):
                pb = dh % 2
                @blk.sync
                def _(s, dh=dh, pb=pb, grp=grp, gw=gw, mg=guard["mm"][pb]):
                    s.wait_ge(cMM.sem, mg)
                    sOL[pb].inc(s.dma_start(out=x1r[pb][:], in_=OBT[dh * 128:(dh + 1) * 128, :]), 16)
                    sWF[pb].inc(s.dma_start(out=wf[pb][:, 0:gw * 128],
                                            in_=WO[dh * 128:(dh + 1) * 128, grp[0] * 128:(grp[0] + gw) * 128]), 16)
                ldw_n = sWF[pb].n
                ldx_n = sOL[pb].n
                @blk.vector
                def _(v, pb=pb, gw=gw, need=ldw_n, wg=guard["wcast"][pb]):
                    v.wait_ge(sWF[pb].sem, need)
                    v.wait_ge(cMM.sem, wg)
                    cDV.inc(v.tensor_copy(out=wb[pb][:, 0:gw * 128], in_=wf[pb][:, 0:gw * 128]))
                dvn = cDV.n
                @blk.tensor
                def _(t, dh=dh, pb=pb, gw=gw, need=dvn, needx=ldx_n, seg=se_gate, dvg=dv_gate):
                    t.wait_ge(cDV.sem, need)
                    t.wait_ge(sOL[pb].sem, needx)
                    if dh == 0:
                        t.wait_ge(cSE.sem, seg)
                        t.wait_ge(cDV.sem, dvg)
                    for mi in range(gw):
                        for tt2 in range(2):
                            cMM.inc(t.matmul(PP[mi * 2 + tt2][:], wb[pb][:, mi * 128:(mi + 1) * 128],
                                             x1r[pb][:, tt2 * 512:(tt2 + 1) * 512],
                                             start=(dh == 0), stop=(dh == 31)))
                guard["mm"][pb] = cMM.n
                guard["wcast"][pb] = cMM.n
            mm_done = cMM.n
            for mi, m in enumerate(grp):
                for tt2 in range(2):
                    ptile = PP[mi * 2 + tt2]
                    @blk.sync
                    def _(s, m=m, tt2=tt2, dv=cDV.n):
                        s.wait_ge(cDV.sem, dv)
                        sRT.inc(s.dma_start(out=rtile2[:], in_=XTO[m * 128:(m + 1) * 128,
                                                                  tt2 * 512:(tt2 + 1) * 512]), 16)
                    rtn = sRT.n
                    @blk.vector
                    def _(v, ptile=ptile, need=mm_done, nrt=rtn, dvp=cDV.n, stp=sX1.n):
                        v.wait_ge(cMM.sem, need)
                        v.wait_ge(sRT.sem, nrt)
                        v.wait_ge(sX1.sem, stp)
                        cDV.inc(v.tensor_add(out=rtile[:], in0=ptile[:], in1=rtile2[:]))
                    dvr = cDV.n
                    @blk.scalar
                    def _(sc, m=m, tt2=tt2, need=dvr):
                        sc.wait_ge(cDV.sem, need)
                        sX1.inc(sc.dma_start(out=R1[m * 128:(m + 1) * 128, tt2 * 512:(tt2 + 1) * 512],
                                             in_=rtile[:]), 16)
                        cSE.inc(sc.activation(out=rtile2[:], in_=rtile[:], func=Act.Square))
                    sqn = cSE.n
                    @blk.tensor
                    def _(t, need=dvr, nsq=sqn, dvp=cDV.n):
                        t.wait_ge(cDV.sem, dvp)
                        t.wait_ge(cDV.sem, need)
                        cMM.inc(t.matmul(PS0[:], onesbF[:], rtile[:], start=True, stop=True))
                        t.wait_ge(cSE.sem, nsq)
                        cMM.inc(t.matmul(PS1[:], onesbF[:], rtile2[:], start=True, stop=True))
                    smn = cMM.n
                    @blk.vector
                    def _(v, tt2=tt2, need=smn):
                        v.wait_ge(cMM.sem, need)
                        cDV.inc(v.tensor_add(out=s1row[0:1, tt2 * 512:(tt2 + 1) * 512],
                                             in0=s1row[0:1, tt2 * 512:(tt2 + 1) * 512], in1=PS0[:]))
                        cDV.inc(v.tensor_add(out=s2row[0:1, tt2 * 512:(tt2 + 1) * 512],
                                             in0=s2row[0:1, tt2 * 512:(tt2 + 1) * 512], in1=PS1[:]))

        @blk.vector
        def _(v):
            v.wait_ge(cDV.sem, cDV.n)
            cDV.inc(v.tensor_scalar(out=mrow[:], in0=s1row[:], scalar1=float(1.0 / 4096.0), scalar2=None,
                                    op0=Alu.mult))
            v.wait_ge(cDV.sem, cDV.n)
            cDV.inc(v.tensor_mul(out=rowa[:, 0:TO], in0=mrow[:], in1=mrow[:]))
            cDV.inc(v.tensor_scalar(out=rowb[:, 0:TO], in0=s2row[:], scalar1=float(1.0 / 4096.0),
                                    scalar2=None, op0=Alu.mult))
            v.wait_ge(cDV.sem, cDV.n)
            cDV.inc(v.tensor_sub(out=rowb[:, 0:TO], in0=rowb[:, 0:TO], in1=rowa[:, 0:TO]))
        vvn = cDV.n
        @blk.vector
        def _(v, need=vvn):
            v.wait_ge(cDV.sem, need)
            cDV.inc(v.tensor_scalar(out=rowb[0:1, 0:TO], in0=rowb[0:1, 0:TO], scalar1=float(EPS),
                                    scalar2=None, op0=Alu.add))
        vvn2 = cDV.n
        @blk.scalar
        def _(sc, need=vvn2):
            sc.wait_ge(cDV.sem, need)
            cSE.inc(sc.activation(out=rowb[0:1, 0:TO], in_=rowb[0:1, 0:TO], func=Act.Sqrt))
        ssn = cSE.n
        @blk.vector
        def _(v, need=ssn):
            v.wait_ge(cSE.sem, need)
            cDV.inc(v.reciprocal(out=rstd[:], in_=rowb[0:1, 0:TO]))
            v.wait_ge(cDV.sem, cDV.n)
            cDV.inc(v.tensor_scalar(out=rowa[:, 0:TO], in0=rowa[:, 0:TO], scalar1=4096.0, scalar2=None,
                                    op0=Alu.mult))
            v.wait_ge(cDV.sem, cDV.n)
            cDV.inc(v.tensor_sub(out=rowa[:, 0:TO], in0=s2row[:], in1=rowa[:, 0:TO]))
            cDV.inc(v.tensor_mul(out=rowb[0:1, 0:TO], in0=rstd[:], in1=rstd[:]))
            v.wait_ge(cDV.sem, cDV.n)
            cDV.inc(v.tensor_mul(out=rowa[:, 0:TO], in0=rowa[:, 0:TO], in1=rowb[0:1, 0:TO]))
        een = cDV.n
        @blk.scalar
        def _(sc, need=een):
            sc.wait_ge(cDV.sem, need)
            cSE.inc(sc.activation(out=erow[:], in_=rowa[0:1, 0:TO], func=Act.Sqrt))
        esn = cSE.n
        @blk.vector
        def _(v, need=esn):
            v.wait_ge(cSE.sem, need)
            cDV.inc(v.tensor_scalar(out=mskrow[:], in0=erow[:], scalar1=64.0, scalar2=None, op0=Alu.is_gt))
        mbn = cDV.n
        for q2 in range(2):
            @blk.tensor
            def _(t, q2=q2, need=mbn, dv=cDV.n):
                t.wait_ge(cDV.sem, max(need, dv))
                cMM.inc(t.matmul(PP[4][:], onesr1F[:], mskrow[0:1, q2 * 512:(q2 + 1) * 512],
                                 start=True, stop=True))
            mqn = cMM.n
            @blk.vector
            def _(v, q2=q2, need=mqn):
                v.wait_ge(cMM.sem, need)
                cDV.inc(v.tensor_copy(out=maskbc[:, q2 * 512:(q2 + 1) * 512], in_=PP[4][:]))

    def p2c_ln1():
        for tt2 in range(2):
            @blk.tensor
            def _(t, tt2=tt2, need=cDV.n):
                t.wait_ge(cDV.sem, need)
                cMM.inc(t.matmul(PP[4][:], onesr1F[:], mrow[0:1, tt2 * 512:(tt2 + 1) * 512],
                                 start=True, stop=True))
                cMM.inc(t.matmul(PP[5][:], onesr1F[:], rstd[0:1, tt2 * 512:(tt2 + 1) * 512],
                                 start=True, stop=True))
            bcn = cMM.n
            for do in range(32):
                @blk.sync
                def _(s, do=do, tt2=tt2, dv=cDV.n, stp=sX1.n):
                    s.wait_ge(cDV.sem, dv)
                    s.wait_ge(sX1.sem, stp)
                    sRT.inc(s.dma_start(out=rtile[:], in_=R1[do * 128:(do + 1) * 128,
                                                           tt2 * 512:(tt2 + 1) * 512]), 16)
                rtn = sRT.n
                @blk.vector
                def _(v, need=rtn, bc=bcn, stp=sX1.n):
                    v.wait_ge(sRT.sem, need)
                    v.wait_ge(cMM.sem, bc)
                    v.wait_ge(sX1.sem, stp)
                    cDV.inc(v.tensor_sub(out=rtile[:], in0=rtile[:], in1=PP[4][:]))
                    v.wait_ge(cDV.sem, cDV.n)
                    cDV.inc(v.tensor_mul(out=rtile[:], in0=rtile[:], in1=PP[5][:]))
                    v.wait_ge(cDV.sem, cDV.n)
                    cDV.inc(v.tensor_copy(out=obuf[:], in_=rtile[:]))
                dvn = cDV.n
                @blk.scalar
                def _(sc, do=do, tt2=tt2, need=dvn):
                    sc.wait_ge(cDV.sem, need)
                    sX1.inc(sc.dma_start(out=X1F[do * 128:(do + 1) * 128, tt2 * 512:(tt2 + 1) * 512],
                                         in_=rtile[:]), 16)
                    sX1.inc(sc.dma_start(out=X1B[do * 128:(do + 1) * 128, tt2 * 512:(tt2 + 1) * 512],
                                         in_=obuf[:]), 16)

    if "dq" in stages:
        dq_make_steps()
    kh_t = kh
    if "p2" in stages:
        p2_attention()
    if "dq" in stages:
        dq_finish()
    es_dq.close()

    def p3_ffn():
        x1res = es.enter_context(nc.sbuf_tensor("x1res", [128, 32 * 1024], BF16))
        ylt = [es.enter_context(nc.sbuf_tensor(f"ylt{i}", [128, 512], F32)) for i in range(6)]
        wtt = [es.enter_context(nc.sbuf_tensor(f"wtt{i}", [128, 384], BF16)) for i in range(2)]
        @blk.sync
        def _(s, agn=cAG.n, x1n=sX1.n):
            s.wait_ge(cAG.sem, agn)
            s.wait_ge(sX1.sem, x1n)
            for ic in range(32):
                sXR.inc(s.dma_start(out=x1res[:, ic * 1024:(ic + 1) * 1024],
                                    in_=X1B[ic * 128:(ic + 1) * 128, :]), 16)
        xrn = sXR.n
        @blk.vector
        def _(v, dv=cDV.n):
            v.wait_ge(cDV.sem, dv)
            cDV.inc(v.memset(s1row[:], 0.0))
            cDV.inc(v.memset(s2row[:], 0.0))
        i0 = 0
        while i0 < 32:
            grp = list(range(i0, min(i0 + 3, 32)))
            i0 += 3
            gw = len(grp)
            ylts = {}
            for tbl in range(2):
                se_gate = cSE.n
                dv_gate = cDV.n
                for ic in range(32):
                    pb = ic % 2
                    @blk.sync
                    def _(s, ic=ic, pb=pb, grp=grp, gw=gw, tbl=tbl, mg=guard["mm"][pb], xn=xrn):
                        s.wait_ge(cMM.sem, mg)
                        s.wait_ge(sXR.sem, xn)
                        sWF[pb].inc(s.dma_start(
                            out=wtt[pb][:, 0:gw * 128],
                            in_=AGO[grp[0] * 128:(grp[0] + gw) * 128,
                                    tbl * 4096 + ic * 128:tbl * 4096 + (ic + 1) * 128],
                            transpose=True), 16)
                    ldw_n = sWF[pb].n
                    @blk.tensor
                    def _(t, ic=ic, pb=pb, gw=gw, need=ldw_n, seg=se_gate, dvg=dv_gate, tbl=tbl):
                        t.wait_ge(sWF[pb].sem, need)
                        if ic == 0:
                            t.wait_ge(cSE.sem, seg)
                            t.wait_ge(cDV.sem, dvg)
                        for mi in range(gw):
                            for tt2 in range(2):
                                cMM.inc(t.matmul(PP[mi * 2 + tt2][:], wtt[pb][:, mi * 128:(mi + 1) * 128],
                                                 x1res[:, ic * 1024 + tt2 * 512:ic * 1024 + (tt2 + 1) * 512],
                                                 start=(ic == 0), stop=(ic == 31)))
                    guard["mm"][pb] = cMM.n
                mm_done = cMM.n
                if tbl == 0:
                    for mi in range(gw):
                        for tt2 in range(2):
                            @blk.scalar
                            def _(sc, k=mi * 2 + tt2, need=mm_done, dv=cDV.n):
                                sc.wait_ge(cMM.sem, need)
                                sc.wait_ge(cDV.sem, dv)
                                cSE.inc(sc.copy(out=ylt[k][:], in_=PP[k][:]))
                else:
                    for mi, m in enumerate(grp):
                        for tt2 in range(2):
                            k = mi * 2 + tt2
                            @blk.sync
                            def _(s, m=m, tt2=tt2, dv=cDV.n, stp=sX1.n):
                                s.wait_ge(cDV.sem, dv)
                                s.wait_ge(sX1.sem, stp)
                                sRT.inc(s.dma_start(out=rtile2[:], in_=X1F[m * 128:(m + 1) * 128,
                                                    tt2 * 512:(tt2 + 1) * 512]), 16)
                            rtn = sRT.n
                            @blk.vector
                            def _(v, k=k, tt2=tt2, need=mm_done, se=cSE.n, stp=sX1.n):
                                v.wait_ge(cMM.sem, need)
                                v.wait_ge(cSE.sem, se)
                                v.wait_ge(sX1.sem, stp)
                                cDV.inc(v.tensor_sub(out=rtile[:], in0=PP[k][:], in1=ylt[k][:]))
                                v.wait_ge(cDV.sem, cDV.n)
                                cDV.inc(v.tensor_mul(out=rtile[:], in0=rtile[:],
                                                     in1=maskbc[:, tt2 * 512:(tt2 + 1) * 512]))
                                v.wait_ge(cDV.sem, cDV.n)
                                cDV.inc(v.tensor_add(out=rtile[:], in0=rtile[:], in1=ylt[k][:]))
                            dvy = cDV.n
                            @blk.vector
                            def _(v, m=m, need=dvy, nrt=rtn):
                                v.wait_ge(cDV.sem, need)
                                v.wait_ge(sRT.sem, nrt)
                                cDV.inc(v.tensor_scalar(out=rtile[:], in0=rtile[:], scalar1=biask[:, m:m + 1],
                                                        scalar2=None, op0=Alu.add))
                                v.wait_ge(cDV.sem, cDV.n)
                                cDV.inc(v.tensor_add(out=rtile[:], in0=rtile[:], in1=rtile2[:]))
                            dvr = cDV.n
                            @blk.scalar
                            def _(sc, m=m, tt2=tt2, need=dvr):
                                sc.wait_ge(cDV.sem, need)
                                sX1.inc(sc.dma_start(out=R2[m * 128:(m + 1) * 128,
                                                            tt2 * 512:(tt2 + 1) * 512], in_=rtile[:]), 16)
                                cSE.inc(sc.activation(out=rtile2[:], in_=rtile[:], func=Act.Square))
                            sqn = cSE.n
                            @blk.tensor
                            def _(t, need=dvr, nsq=sqn, dvp=cDV.n):
                                t.wait_ge(cDV.sem, dvp)
                                cMM.inc(t.matmul(PS0[:], onesbF[:], rtile[:], start=True, stop=True))
                                t.wait_ge(cSE.sem, nsq)
                                cMM.inc(t.matmul(PS1[:], onesbF[:], rtile2[:], start=True, stop=True))
                            smn = cMM.n
                            @blk.vector
                            def _(v, tt2=tt2, need=smn):
                                v.wait_ge(cMM.sem, need)
                                cDV.inc(v.tensor_add(out=s1row[0:1, tt2 * 512:(tt2 + 1) * 512],
                                                     in0=s1row[0:1, tt2 * 512:(tt2 + 1) * 512], in1=PS0[:]))
                                cDV.inc(v.tensor_add(out=s2row[0:1, tt2 * 512:(tt2 + 1) * 512],
                                                     in0=s2row[0:1, tt2 * 512:(tt2 + 1) * 512], in1=PS1[:]))
        @blk.vector
        def _(v):
            v.wait_ge(cDV.sem, cDV.n)
            cDV.inc(v.tensor_scalar(out=m2row[:], in0=s1row[:], scalar1=float(1.0 / 4096.0), scalar2=None,
                                    op0=Alu.mult))
            v.wait_ge(cDV.sem, cDV.n)
            cDV.inc(v.tensor_mul(out=rowa[:, 0:TO], in0=m2row[:], in1=m2row[:]))
            cDV.inc(v.tensor_scalar(out=rowb[:, 0:TO], in0=s2row[:], scalar1=float(1.0 / 4096.0),
                                    scalar2=None, op0=Alu.mult))
            v.wait_ge(cDV.sem, cDV.n)
            cDV.inc(v.tensor_sub(out=rowb[:, 0:TO], in0=rowb[:, 0:TO], in1=rowa[:, 0:TO]))
            v.wait_ge(cDV.sem, cDV.n)
            cDV.inc(v.tensor_scalar(out=rowb[0:1, 0:TO], in0=rowb[0:1, 0:TO], scalar1=float(EPS),
                                    scalar2=None, op0=Alu.add))
        vvn = cDV.n
        @blk.scalar
        def _(sc, need=vvn):
            sc.wait_ge(cDV.sem, need)
            cSE.inc(sc.activation(out=rowb[0:1, 0:TO], in_=rowb[0:1, 0:TO], func=Act.Sqrt))
        ssn = cSE.n
        @blk.vector
        def _(v, need=ssn):
            v.wait_ge(cSE.sem, need)
            cDV.inc(v.reciprocal(out=r2row[:], in_=rowb[0:1, 0:TO]))

    def p4_out():
        for tt2 in range(2):
            @blk.tensor
            def _(t, tt2=tt2, need=cDV.n, ses=cSE.n):
                t.wait_ge(cDV.sem, need)
                t.wait_ge(cSE.sem, ses)
                cMM.inc(t.matmul(PP[4][:], onesr1F[:], m2row[0:1, tt2 * 512:(tt2 + 1) * 512],
                                 start=True, stop=True))
                cMM.inc(t.matmul(PP[5][:], onesr1F[:], r2row[0:1, tt2 * 512:(tt2 + 1) * 512],
                                 start=True, stop=True))
            bcn = cMM.n
            for do in range(32):
                @blk.sync
                def _(s, do=do, tt2=tt2, dv=cDV.n, stp=sOUT.n, sx=sX1.n):
                    s.wait_ge(cDV.sem, dv)
                    s.wait_ge(sOUT.sem, stp)
                    s.wait_ge(sX1.sem, sx)
                    sRT.inc(s.dma_start(out=rtile[:], in_=R2[do * 128:(do + 1) * 128,
                                                           tt2 * 512:(tt2 + 1) * 512]), 16)
                rtn = sRT.n
                @blk.vector
                def _(v, need=rtn, bc=bcn):
                    v.wait_ge(sRT.sem, need)
                    v.wait_ge(cMM.sem, bc)
                    cDV.inc(v.tensor_sub(out=rtile[:], in0=rtile[:], in1=PP[4][:]))
                    v.wait_ge(cDV.sem, cDV.n)
                    cDV.inc(v.tensor_mul(out=rtile[:], in0=rtile[:], in1=PP[5][:]))
                dvn = cDV.n
                @blk.scalar
                def _(sc, do=do, tt2=tt2, need=dvn):
                    sc.wait_ge(cDV.sem, need)
                    sOUT.inc(sc.dma_start(out=OUT[do * 128:(do + 1) * 128, tt2 * 512:(tt2 + 1) * 512],
                                          in_=rtile[:]), 16)

    if "p2b" in stages:
        p2b_wo()
    if "p2c" in stages:
        p2c_ln1()
    if "p3" in stages:
        p3_ffn()
    if "p4" in stages:
        p4_out()

    if dbg:
        @blk.sync
        def _(s):
            for c in ALL_CTRS:
                if c.n:
                    s.wait_ge(c.sem, c.n)
            for name, dst in dbg_map.items():
                src = dram[name]
                rows = src.shape[0]
                step = max(1, (1 << 20) // int(np.prod(src.shape[1:]) * 4))
                r = 0
                while r < rows:
                    rr = min(step, rows - r)
                    cLD.inc(s.dma_start(out=dst[r:r + rr], in_=src[r:r + rr]), 16)
                    r += rr
            s.wait_ge(cLD.sem, cLD.n)

    @blk.sync
    def _(s):
        for c in ALL_CTRS:
            if c.n:
                s.wait_ge(c.sem, c.n)

    es.close()
    return nc, dram


def rope_tables_np():
    import jax
    with jax.default_device(jax.devices("cpu")[0]):
        import jax.numpy as jnp
        half = 64
        inv_freq = 1.0 / 10000.0 ** (jnp.arange(0, half, dtype=jnp.float32) / half)
        pos = jnp.arange(2048, dtype=jnp.float32)
        ang = pos[:, None] * inv_freq[None, :]
        cos = np.asarray(jnp.cos(ang))
        sin = np.asarray(jnp.sin(ang))
    cosfm = np.ascontiguousarray(np.concatenate([cos.T, cos.T], axis=0).astype(np.float32))
    sinfm = np.ascontiguousarray(np.concatenate([-sin.T, sin.T], axis=0).astype(np.float32))
    return cosfm, sinfm


def host_inputs(core, X, Wq, Wk, Wv, Wo, qweight, lut4, lut8, bias):
    b, half = core >> 1, core & 1
    cosfm, sinfm = rope_tables_np()
    own_cols = np.concatenate([np.arange((2 * j + half) * 128, (2 * j + half) * 128 + 128) for j in range(8)])
    XTc = np.ascontiguousarray(X[b].T.astype(np.float32))
    XTOc = np.ascontiguousarray(XTc[:, own_cols])
    tk = np.arange(128)[:, None]
    tq = np.arange(128)[None, :]
    if half == 0:
        m0 = np.where(tk <= tq, 0.0, -1e30)
        m1 = np.full((128, 128), -1e30)
    else:
        m0 = np.zeros((128, 128))
        m1 = np.where(tk <= tq, 0.0, -1e30)
    masks = np.ascontiguousarray(np.stack([m0, m1]).astype(np.float32))
    qws = np.ascontiguousarray(
        qweight[:, 512 * core:512 * (core + 1), :].reshape(8, 4, 128, 128).transpose(0, 2, 1, 3))
    lut4s = np.ascontiguousarray(lut4[512 * core:512 * (core + 1)].reshape(4, 128, 16).transpose(1, 0, 2))
    lut8s = np.ascontiguousarray(lut8[512 * core:512 * (core + 1)].reshape(4, 128, 256).transpose(1, 0, 2))
    biasr = np.ascontiguousarray(bias.reshape(32, 128).T.astype(np.float32))
    return {
        "XT": XTc, "XTO": XTOc,
        "WQ": np.ascontiguousarray(Wq.astype(np.float32)),
        "WK": np.ascontiguousarray(Wk.astype(np.float32)),
        "WV": np.ascontiguousarray(Wv.astype(np.float32)),
        "WO": np.ascontiguousarray(Wo.astype(np.float32)),
        "QWS": qws, "LUT4S": lut4s, "LUT8S": lut8s, "BIASR": biasr,
        "COSO": np.ascontiguousarray(cosfm[:, own_cols]), "SINO": np.ascontiguousarray(sinfm[:, own_cols]),
        "COSC": cosfm, "SINC": sinfm, "MASKS": masks,
    }


from concourse.bass_utils import run_bass_kernel_spmd

_BUILD_CACHE = {}

def kernel(L0_attn_X, L0_attn_Wq, L0_attn_Wk, L0_attn_Wv, L0_attn_Wo,
           L0_ffn_qweight, L0_ffn_lut4, L0_ffn_lut8, L0_ffn_bias):
    X = np.asarray(L0_attn_X)
    Wq = np.asarray(L0_attn_Wq); Wk = np.asarray(L0_attn_Wk)
    Wv = np.asarray(L0_attn_Wv); Wo = np.asarray(L0_attn_Wo)
    qw = np.asarray(L0_ffn_qweight)
    lut4 = np.asarray(L0_ffn_lut4); lut8 = np.asarray(L0_ffn_lut8)
    bias = np.asarray(L0_ffn_bias)

    nc, dram = build(stages=("p1", "dq", "p2", "p2b", "p2c", "p3", "p4"))
    in_maps = [host_inputs(c, X, Wq, Wk, Wv, Wo, qw, lut4, lut8, bias) for c in range(8)]
    res = run_bass_kernel_spmd(nc, in_maps, core_ids=list(range(8)))
    out = np.empty((4, 2048, 4096), dtype=np.float32)
    for c in range(8):
        b, half = c >> 1, c & 1
        oc = np.asarray(res.results[c]["OUT"])
        for j in range(8):
            out[b, (2 * j + half) * 128:(2 * j + half) * 128 + 128, :] = \
                oc[:, j * 128:(j + 1) * 128].T
    return out


# version 14
# speedup vs baseline: 1.0572x; 1.0572x over previous
import contextlib
import numpy as np
import ml_dtypes
import concourse.bass as bass
import concourse.mybir as mybir

F32 = mybir.dt.float32
BF16 = mybir.dt.bfloat16
I32 = mybir.dt.int32
U16 = mybir.dt.uint16
U8 = mybir.dt.uint8
Alu = mybir.AluOpType
Act = mybir.ActivationFunctionType

D = 4096
TC = 2048
TO = 1024
SC = float(1.0 / np.float64(np.float32(np.sqrt(np.float32(128.0)))))
EPS = 1e-5
HN = 1024

class Ctr:
    def __init__(self, es, nc, name):
        self.sem = es.enter_context(nc.semaphore(name))
        self.n = 0
    def inc(self, instr, k=1):
        self.n += k
        instr.then_inc(self.sem, k)
        return instr


def build(stages, dbg=(), fake_ag=False):
    nc = bass.Bass("TRN2", target_bir_lowering=False, debug=False)
    es = contextlib.ExitStack()
    dram = {}

    def din(name, shape, dt):
        dram[name] = nc.dram_tensor(name, shape, dt, kind="ExternalInput").ap(); return dram[name]
    def dout(name, shape, dt):
        dram[name] = nc.dram_tensor(name, shape, dt, kind="ExternalOutput").ap(); return dram[name]
    def dint(name, shape, dt):
        dram[name] = nc.dram_tensor(name, shape, dt, kind="Internal").ap(); return dram[name]
    def sb(name, shape, dt):
        return es.enter_context(nc.sbuf_tensor(name, shape, dt))
    def psum(name, shape):
        return es.enter_context(nc.psum_tensor(name, shape, F32))

    XT = din("XT", [D, TC], F32)
    XTO = din("XTO", [D, TO], F32)
    WQ = din("WQ", [D, D], F32)
    WK = din("WK", [D, 1024], F32)
    WV = din("WV", [D, 1024], F32)
    WO = din("WO", [D, D], F32)
    QWS = din("QWS", [8, 128, 4, 128], I32)
    LUT4S = din("LUT4S", [128, 4, 16], F32)
    LUT8S = din("LUT8S", [128, 4, 256], F32)
    BIASR = din("BIASR", [128, 32], F32)
    COSO = din("COSO", [128, TO], F32)
    SINO = din("SINO", [128, TO], F32)
    COSC = din("COSC", [128, TC], F32)
    SINC = din("SINC", [128, TC], F32)
    MASKB = din("MASKB", [2, 128, 128], BF16)
    IDENT = din("IDENT", [128, 128], BF16)
    OUT = dout("OUT", [D, TO], F32)

    XTb = dint("XTb", [D, TC], BF16)
    XTOb = dint("XTOb", [D, TO], BF16)
    QT = dint("QTs", [D, TO], BF16)
    KT = dint("KTs", [1024, TC], BF16)
    VS = dint("VSs", [TC, 1024], BF16)
    OBT = dint("OBT", [D, TO], BF16)
    R1 = dint("R1", [D, TO], F32)
    X1F = dint("X1F", [D, TO], F32)
    X1B = dint("X1B", [D, TO], BF16)
    R2 = dint("R2", [D, TO], F32)
    AGI = dint("AGI", [512, 8192], BF16)
    AGO = dint("AGO", [D, 8192], BF16)

    dbg_map = {}
    for name in dbg:
        src = dram[name]
        dbg_map[name] = dout("DBG_" + name, list(src.shape), src.dtype)

    PP = [psum(f"pp{i}", [128, 512]) for i in range(6)]
    PS0 = psum("ps0", [1, 512])
    PS1 = psum("ps1", [1, 512])

    wf = [sb(f"wf{i}", [128, 512], F32) for i in range(2)]
    wb = [sb(f"wb{i}", [128, 512], BF16) for i in range(2)]
    xb = [sb(f"xb{i}", [128, TC], BF16) for i in range(2)]
    qraw = sb("qraw", [128, TC], F32)
    qswp = sb("qswp", [128, TC], F32)
    bst = sb("bst", [128, TC], BF16)
    cosO = sb("cosO", [128, TO], F32)
    sinO = sb("sinO", [128, TO], F32)
    cosC = sb("cosC", [128, TC], F32)
    sinC = sb("sinC", [128, TC], F32)
    masktb = sb("masktb", [128, 256], BF16)
    identb = sb("identb", [128, 128], BF16)
    biask = sb("biask", [128, 32], F32)

    qh = sb("qh", [128, TO], BF16)
    kh = sb("kh", [128, TC], BF16)
    vh = sb("vh", [128, TC], BF16)
    at_ = [sb(f"at{i}", [128, 512], BF16) for i in range(2)]
    sqs = qswp
    onesb = sb("onesb", [128, 1], BF16)
    onesr1 = sb("onesr1", [1, 128], BF16)
    onesr1F = sb("onesr1F", [1, 128], F32)
    rowsA = sb("rowsA", [1, 5376], F32)
    rowa = rowsA[0:1, 0:TC]
    rowb = rowsA[0:1, TC:TC + 1280]
    negMf = rowsA[0:1, 3328:4352]
    negMb = sb("negMb", [1, TO], BF16)
    rrow = sb("rrow", [1, 512], F32)
    obuf = sb("obuf", [128, 512], BF16)
    s1row = sb("s1row", [1, TO], F32)
    s2row = sb("s2row", [1, TO], F32)
    mrow = sb("mrow", [1, TO], F32)
    rstd = sb("rstd", [1, TO], F32)
    mskrow = sb("mskrow", [1, TO], F32)
    erow = rowsA[0:1, 4352:5376]
    m2row = mrow
    r2row = rstd
    maskbc = sb("maskbc", [128, TO], F32)
    rtile = sb("rtile", [128, 512], F32)
    rtile2 = sb("rtile2", [128, 512], F32)
    x1r = [sb(f"x1r{i}", [128, TO], BF16) for i in range(2)]
    onesbF = sb("onesbF", [128, 1], F32)

    es_dq = contextlib.ExitStack()
    def sbq(name, shape, dt):
        return es_dq.enter_context(nc.sbuf_tensor(name, shape, dt))
    dq_qw8 = sbq("dq_qw8", [128, 8 * 512], U8)
    dq_l4 = sbq("dq_l4", [128, 16], F32)
    dq_l8 = sbq("dq_l8", [128, 256], F32)
    dq_idx = sbq("dq_idx", [128, 4096], U16)
    dq_ci = sbq("dq_ci", [128, HN], I32)
    dq_t8 = sbq("dq_t8", [128, 512], U8)
    dq_wl = sbq("dq_wl", [128, HN], BF16)
    dq_wlt = sbq("dq_wlt", [128, HN], BF16)
    dq_wh = sbq("dq_wh", [128, HN], BF16)
    dq_V16 = sbq("dq_V16", [128, 256], I32)
    dq_W32 = sbq("dq_W32", [128, 4 * 32], I32)
    dq_XW = sbq("dq_XW", [128, 4 * 16], I32)
    dq_msk = sbq("dq_msk", [128, 5 * HN], I32)
    dq_mux = sbq("dq_mux", [128, 5 * HN], I32)
    dq_sham = sbq("dq_sham", [128, HN], I32)
    dq_acc = dq_ci
    dq_t32 = sbq("dq_t32", [128, HN], I32)

    cLD = Ctr(es, nc, "cLD")
    sCONST = Ctr(es, nc, "sCONST")
    sXB = [Ctr(es, nc, "sXB0"), Ctr(es, nc, "sXB1")]
    sWF = [Ctr(es, nc, "sWF0"), Ctr(es, nc, "sWF1")]
    sSW = Ctr(es, nc, "sSW")
    sBST = Ctr(es, nc, "sBST")
    sOBUF = Ctr(es, nc, "sOBUF")
    sQH = Ctr(es, nc, "sQH")
    sKH = Ctr(es, nc, "sKH")
    sVH = Ctr(es, nc, "sVH")
    sOB = Ctr(es, nc, "sOB")
    sOL = [Ctr(es, nc, "sOL0"), Ctr(es, nc, "sOL1")]
    sRT = Ctr(es, nc, "sRT")
    sX1 = Ctr(es, nc, "sX1")
    cST = Ctr(es, nc, "cST")
    cGD = Ctr(es, nc, "cGD")
    cMM = Ctr(es, nc, "cMM")
    cSE = Ctr(es, nc, "cSE")
    cDV = Ctr(es, nc, "cDV")
    cAG = Ctr(es, nc, "cAG")
    sDQ = Ctr(es, nc, "sDQ")
    sDQS = Ctr(es, nc, "sDQS")
    sXR = Ctr(es, nc, "sXR")
    sOUT = Ctr(es, nc, "sOUT")
    ALL_CTRS = [cLD, sCONST, sSW, sBST, sOBUF, sQH, sKH, sVH, sOB, sRT, sX1, cST, cGD, cMM, cSE, cDV, cAG, sDQ, sDQS, sXR, sOUT] + sXB + sWF + sOL

    blk = es.enter_context(nc.Block())

    @blk.sync
    def _(s):
        sCONST.inc(s.dma_start(out=cosO[:], in_=COSO[:]), 16)
        sCONST.inc(s.dma_start(out=sinO[:], in_=SINO[:]), 16)
        sCONST.inc(s.dma_start(out=cosC[:], in_=COSC[:]), 16)
        sCONST.inc(s.dma_start(out=sinC[:], in_=SINC[:]), 16)
        sCONST.inc(s.dma_start(out=masktb[:, 0:128], in_=MASKB[0]), 16)
        sCONST.inc(s.dma_start(out=masktb[:, 128:256], in_=MASKB[1]), 16)
        sCONST.inc(s.dma_start(out=identb[:], in_=IDENT[:]), 16)
        sCONST.inc(s.dma_start(out=biask[:], in_=BIASR[:]), 16)

    @blk.vector
    def _(v):
        cDV.inc(v.memset(onesb[:], 1.0))
        cDV.inc(v.memset(onesr1[:], 1.0))
        cDV.inc(v.memset(onesr1F[:], 1.0))
        cDV.inc(v.memset(onesbF[:], 1.0))
        v.wait_ge(sCONST.sem, sCONST.n)

    if "p1" in stages:
        @blk.gpsimd
        def _(g):
            for i in range(8):
                cGD.inc(g.dma_start(out=XTb[i * 512:(i + 1) * 512, :], in_=XT[i * 512:(i + 1) * 512, :]), 16)
            for i in range(4):
                cGD.inc(g.dma_start(out=XTOb[i * 1024:(i + 1) * 1024, :], in_=XTO[i * 1024:(i + 1) * 1024, :]), 16)
    n_cast = cGD.n

    guard = {"mm": [0, 0], "wcast": [0, 0]}

    def proj_std(MCH, TN, W, XSRC, OUTT, cost, sint, G):
        ntt = TN // 512
        i0 = 0
        while i0 < MCH:
            grp = list(range(i0, min(i0 + G, MCH)))
            i0 += G
            gw = len(grp)
            se_gate = cSE.n
            for ic in range(32):
                pb = ic % 2
                @blk.sync
                def _(s, ic=ic, pb=pb, grp=grp, gw=gw, mg=guard["mm"][pb], W=W, XSRC=XSRC, TN=TN):
                    s.wait_ge(cMM.sem, mg)
                    sXB[pb].inc(s.dma_start(out=xb[pb][:, 0:TN], in_=XSRC[ic * 128:(ic + 1) * 128, :]), 16)
                    sWF[pb].inc(s.dma_start(out=wf[pb][:, 0:gw * 128],
                                        in_=W[ic * 128:(ic + 1) * 128, grp[0] * 128:(grp[0] + gw) * 128]), 16)
                ldw_n = sWF[pb].n
                ldx_n = sXB[pb].n
                @blk.vector
                def _(v, pb=pb, gw=gw, need=ldw_n, wg=guard["wcast"][pb]):
                    v.wait_ge(sWF[pb].sem, need)
                    v.wait_ge(cMM.sem, wg)
                    cDV.inc(v.tensor_copy(out=wb[pb][:, 0:gw * 128], in_=wf[pb][:, 0:gw * 128]))
                dvn = cDV.n
                @blk.tensor
                def _(t, ic=ic, pb=pb, gw=gw, ntt=ntt, need=dvn, needx=ldx_n, seg=se_gate):
                    t.wait_ge(cDV.sem, need)
                    t.wait_ge(sXB[pb].sem, needx)
                    if ic == 0:
                        t.wait_ge(cSE.sem, seg)
                    for mi in range(gw):
                        for tt in range(ntt):
                            cMM.inc(t.matmul(PP[mi * ntt + tt][:],
                                             wb[pb][:, mi * 128:(mi + 1) * 128],
                                             xb[pb][:, tt * 512:(tt + 1) * 512],
                                             start=(ic == 0), stop=(ic == 31)))
                guard["mm"][pb] = cMM.n
                guard["wcast"][pb] = cMM.n
                if "dq" in stages and ic % 8 == 7:
                    dq_pump(1)
            mm_done = cMM.n
            for mi, m in enumerate(grp):
                for tt in range(ntt):
                    @blk.scalar
                    def _(sc, pt=PP[mi * ntt + tt], toff=tt * 512, need=mm_done, dn=cDV.n):
                        sc.wait_ge(cMM.sem, need)
                        sc.wait_ge(cDV.sem, dn)
                        cSE.inc(sc.copy(out=qraw[:, toff:toff + 512], in_=pt[:]))
                sen = cSE.n
                @blk.sync
                def _(s, TN=TN, need=sen, dvp=cDV.n):
                    s.wait_ge(cSE.sem, need)
                    s.wait_ge(cDV.sem, dvp)
                    sSW.inc(s.dma_start(out=qswp[0:64, 0:TN], in_=qraw[64:128, 0:TN]), 16)
                    sSW.inc(s.dma_start(out=qswp[64:128, 0:TN], in_=qraw[0:64, 0:TN]), 16)
                ldn2 = sSW.n
                @blk.vector
                def _(v, TN=TN, cost=cost, sint=sint, needL=ldn2, needS=sen):
                    v.wait_ge(sSW.sem, needL)
                    v.wait_ge(cSE.sem, needS)
                    cDV.inc(v.tensor_mul(out=qraw[:, 0:TN], in0=qraw[:, 0:TN], in1=cost[:, 0:TN]))
                    cDV.inc(v.tensor_mul(out=qswp[:, 0:TN], in0=qswp[:, 0:TN], in1=sint[:, 0:TN]))
                    v.wait_ge(cDV.sem, cDV.n)
                    cDV.inc(v.tensor_add(out=qraw[:, 0:TN], in0=qraw[:, 0:TN], in1=qswp[:, 0:TN]))
                    v.wait_ge(cDV.sem, cDV.n)
                    cDV.inc(v.tensor_copy(out=bst[:, 0:TN], in_=qraw[:, 0:TN]))
                dvn3 = cDV.n
                @blk.scalar
                def _(sc, m=m, TN=TN, OUTT=OUTT, need=dvn3):
                    sc.wait_ge(cDV.sem, need)
                    sBST.inc(sc.dma_start(out=OUTT[m * 128:(m + 1) * 128, 0:TN], in_=bst[:, 0:TN]), 16)
                stn = sBST.n
                @blk.vector
                def _(v, need=stn):
                    v.wait_ge(sBST.sem, need)

    def proj_v():
        i0 = 0
        while i0 < 16:
            grp = list(range(i0, min(i0 + 3, 16)))
            i0 += 3
            gw = len(grp)
            se_gate = cSE.n
            for half in range(2):
                for ic in range(32):
                    pb = ic % 2
                    @blk.sync
                    def _(s, ic=ic, pb=pb, half=half, mg=guard["mm"][pb]):
                        s.wait_ge(cMM.sem, mg)
                        sXB[pb].inc(s.dma_start(out=xb[pb][:, 0:TC], in_=XTb[ic * 128:(ic + 1) * 128, :]), 16)
                        sWF[pb].inc(s.dma_start(out=wf[pb][:, 0:512],
                                            in_=WV[ic * 128:(ic + 1) * 128, half * 512:(half + 1) * 512]), 16)
                    ldw_n = sWF[pb].n
                    ldx_n = sXB[pb].n
                    @blk.vector
                    def _(v, pb=pb, need=ldw_n, wg=guard["wcast"][pb]):
                        v.wait_ge(sWF[pb].sem, need)
                        v.wait_ge(cMM.sem, wg)
                        cDV.inc(v.tensor_copy(out=wb[pb][:, 0:512], in_=wf[pb][:, 0:512]))
                    dvn = cDV.n
                    @blk.tensor
                    def _(t, ic=ic, pb=pb, grp=grp, gw=gw, half=half, need=dvn, needx=ldx_n, seg=se_gate):
                        t.wait_ge(cDV.sem, need)
                        t.wait_ge(sXB[pb].sem, needx)
                        if ic == 0:
                            t.wait_ge(cSE.sem, seg)
                        for mi, m in enumerate(grp):
                            cMM.inc(t.matmul(PP[mi * 2 + half][:], xb[pb][:, m * 128:(m + 1) * 128],
                                             wb[pb][:, 0:512], start=(ic == 0), stop=(ic == 31)))
                    guard["mm"][pb] = cMM.n
                    guard["wcast"][pb] = cMM.n
            mm_done = cMM.n
            last_st = [sOBUF.n]
            for mi, m in enumerate(grp):
                for half in range(2):
                    @blk.scalar
                    def _(sc, pt=PP[mi * 2 + half], need=mm_done, lst=last_st[0]):
                        sc.wait_ge(cMM.sem, need)
                        sc.wait_ge(sOBUF.sem, lst)
                        cSE.inc(sc.copy(out=obuf[:], in_=pt[:]))
                    sen = cSE.n
                    @blk.scalar
                    def _(sc, m=m, half=half, need=sen):
                        sc.wait_ge(cSE.sem, need)
                        sOBUF.inc(sc.dma_start(out=VS[m * 128:(m + 1) * 128, half * 512:(half + 1) * 512],
                                             in_=obuf[:]), 16)
                    last_st[0] = sOBUF.n




    FXS = 16
    dq_state = {"steps": [], "i": 0}
    def dq_make_steps():
        steps = dq_state["steps"]
        lut_gate = [0]
        for ch in range(4):
            def load_chunk(ch=ch):
                @blk.gpsimd
                def _(g, ch=ch, dv=cDV.n):
                    g.wait_ge(cDV.sem, dv)
                    for p in range(8):
                        sDQ.inc(g.dma_start(out=dq_qw8[:, p * 512:(p + 1) * 512],
                                            in_=QWS[p, :, ch, :].bitcast(U8)), 16)
                    sDQ.inc(g.dma_start(out=dq_l4[:], in_=LUT4S[:, ch, :]), 16)
                    sDQ.inc(g.dma_start(out=dq_l8[:], in_=LUT8S[:, ch, :]), 16)
            steps.append(load_chunk)
            ldq_after = None
            def ext_init(ch=ch):
                @blk.vector
                def _(v, need=sDQ.n):
                    v.wait_ge(sDQ.sem, need)
                    cDV.inc(v.memset(dq_idx[:], 0))
                    v.wait_ge(cDV.sem, cDV.n)
            steps.append(ext_init)
            for p in range(8):
                def ext_plane(p=p):
                    @blk.vector
                    def _(v, p=p):
                        for r in range(8):
                            m = (1 << r) * 0x01010101
                            if m >= 2 ** 31:
                                m -= 2 ** 32
                            cDV.inc(v.tensor_scalar(out=dq_t8.bitcast(I32)[:, 0:128],
                                                    in0=dq_qw8.bitcast(I32)[:, p * 128:(p + 1) * 128],
                                                    scalar1=m, scalar2=None, op0=Alu.bitwise_and))
                            v.wait_ge(cDV.sem, cDV.n)
                            sl = dq_idx[:].rearrange("o (c r) -> o c r", r=8)[:, :, r]
                            cDV.inc(v.scalar_tensor_tensor(out=sl, in0=dq_t8[:], scalar=float(2.0 ** (p - r)),
                                                           in1=sl, op0=Alu.mult, op1=Alu.add))
                            v.wait_ge(cDV.sem, cDV.n)
                steps.append(ext_plane)
            def tprep(ch=ch):
                @blk.vector
                def _(v):
                    cDV.inc(v.tensor_scalar(out=dq_V16[:], in0=dq_l8[:], scalar1=float(2.0 ** FXS),
                                            scalar2=32768.0, op0=Alu.mult, op1=Alu.add))
                    v.wait_ge(cDV.sem, cDV.n)
                    for sl4 in range(4):
                        base = sl4 * 32
                        cDV.inc(v.tensor_scalar(out=dq_t32[:, 0:256], in0=dq_V16[:], scalar1=4 * sl4,
                                                scalar2=15, op0=Alu.logical_shift_right, op1=Alu.bitwise_and))
                        v.wait_ge(cDV.sem, cDV.n)
                        for k in range(8):
                            sl = dq_t32[:, 0:256].rearrange("o (j k) -> o j k", k=8)[:, :, k]
                            if k == 0:
                                cDV.inc(v.tensor_copy(out=dq_W32[:, base:base + 32], in_=sl))
                            else:
                                cDV.inc(v.tensor_scalar(out=dq_t32[:, 256:288], in0=sl, scalar1=4 * k,
                                                        scalar2=None, op0=Alu.logical_shift_left))
                                v.wait_ge(cDV.sem, cDV.n)
                                cDV.inc(v.tensor_tensor(out=dq_W32[:, base:base + 32],
                                                        in0=dq_W32[:, base:base + 32],
                                                        in1=dq_t32[:, 256:288], op=Alu.bitwise_or))
                            v.wait_ge(cDV.sem, cDV.n)
                    for sl4 in range(4):
                        a = dq_W32[:, sl4 * 32:(sl4 + 1) * 32].rearrange("o (m e) -> o m e", e=2)
                        cDV.inc(v.tensor_tensor(out=dq_XW[:, sl4 * 16:(sl4 + 1) * 16], in0=a[:, :, 0],
                                                in1=a[:, :, 1], op=Alu.bitwise_xor))
                    v.wait_ge(cDV.sem, cDV.n)
            steps.append(tprep)
            for q in range(4):
                def quarter_pre(ch=ch, q=q):
                    @blk.vector
                    def _(v, q=q, dv=None):
                        v.wait_ge(cDV.sem, cDV.n)
                        cDV.inc(v.tensor_copy(out=dq_ci[:], in_=dq_idx[:, q * HN:(q + 1) * HN]))
                        v.wait_ge(cDV.sem, cDV.n)
                        cDV.inc(v.tensor_scalar(out=dq_msk[:, 0:HN], in0=dq_ci[:], scalar1=8, scalar2=None,
                                                op0=Alu.bitwise_and))
                        v.wait_ge(cDV.sem, cDV.n)
                        cDV.inc(v.tensor_scalar(out=dq_msk[:, 0:HN], in0=dq_msk[:, 0:HN], scalar1=-0.125,
                                                scalar2=None, op0=Alu.mult))
                        for k in range(1, 5):
                            cDV.inc(v.tensor_scalar(out=dq_msk[:, k * HN:(k + 1) * HN], in0=dq_ci[:],
                                                    scalar1=(1 << (k + 3)), scalar2=None, op0=Alu.bitwise_and))
                        cDV.inc(v.tensor_scalar(out=dq_sham[:], in0=dq_ci[:], scalar1=7, scalar2=None,
                                                op0=Alu.bitwise_and))
                        v.wait_ge(cDV.sem, cDV.n)
                        cDV.inc(v.tensor_scalar(out=dq_sham[:], in0=dq_sham[:], scalar1=4.0, scalar2=None,
                                                op0=Alu.mult))
                        cDV.inc(v.tensor_scalar(out=dq_t32[:], in0=dq_ci[:], scalar1=15, scalar2=None,
                                                op0=Alu.bitwise_and))
                        v.wait_ge(cDV.sem, cDV.n)
                    @blk.vector
                    def _(v):
                        for c in range(16):
                            cDV.inc(v.tensor_scalar(out=dq_wlt[:], in0=dq_t32[:], scalar1=float(c),
                                                    scalar2=dq_l4[:, c:c + 1], op0=Alu.is_equal, op1=Alu.mult))
                            v.wait_ge(cDV.sem, cDV.n)
                            if c == 0:
                                cDV.inc(v.tensor_copy(out=dq_wl[:], in_=dq_wlt[:]))
                            else:
                                cDV.inc(v.tensor_tensor(out=dq_wl[:], in0=dq_wl[:], in1=dq_wlt[:], op=Alu.add))
                            v.wait_ge(cDV.sem, cDV.n)
                    @blk.gpsimd
                    def _(g, ch=ch, q=q, dv=cDV.n):
                        g.wait_ge(cDV.sem, dv)
                        sDQS.inc(g.dma_start(out=AGI[ch * 128:(ch + 1) * 128, q * HN:(q + 1) * HN],
                                             in_=dq_wl[:]), 16)
                steps.append(quarter_pre)
                def mslot(j):
                    return dq_mux[:, j * HN:(j + 1) * HN]
                for sl4 in range(4):
                    def mux_slice(ch=ch, q=q, sl4=sl4):
                        @blk.vector
                        def _(v, sl4=sl4):
                            filled = [False] * 4
                            for mm_ in range(16):
                                cur = 4
                                cDV.inc(v.tensor_scalar(out=mslot(cur), in0=dq_msk[:, 0:HN],
                                                        scalar1=dq_XW[:, sl4 * 16 + mm_:sl4 * 16 + mm_ + 1],
                                                        scalar2=dq_W32[:, sl4 * 32 + 2 * mm_:sl4 * 32 + 2 * mm_ + 1],
                                                        op0=Alu.bitwise_and, op1=Alu.bitwise_xor))
                                v.wait_ge(cDV.sem, cDV.n)
                                lev = 1
                                while lev <= 4 and filled[lev - 1]:
                                    cDV.inc(v.copy_predicated(out=mslot(lev - 1),
                                                              mask=dq_msk[:, lev * HN:(lev + 1) * HN],
                                                              data=mslot(cur)))
                                    v.wait_ge(cDV.sem, cDV.n)
                                    filled[lev - 1] = False
                                    cur = lev - 1
                                    lev += 1
                                if lev <= 4:
                                    if cur != lev - 1:
                                        cDV.inc(v.tensor_copy(out=mslot(lev - 1), in_=mslot(cur)))
                                        v.wait_ge(cDV.sem, cDV.n)
                                    filled[lev - 1] = True
                            res = mslot(3)
                            if sl4 == 0:
                                cDV.inc(v.tensor_tensor(out=dq_acc[:], in0=res, in1=dq_sham[:],
                                                        op=Alu.logical_shift_right))
                                v.wait_ge(cDV.sem, cDV.n)
                                cDV.inc(v.tensor_scalar(out=dq_acc[:], in0=dq_acc[:], scalar1=15,
                                                        scalar2=None, op0=Alu.bitwise_and))
                            else:
                                cDV.inc(v.tensor_tensor(out=dq_t32[:], in0=res, in1=dq_sham[:],
                                                        op=Alu.logical_shift_right))
                                v.wait_ge(cDV.sem, cDV.n)
                                cDV.inc(v.tensor_scalar(out=dq_t32[:], in0=dq_t32[:], scalar1=15,
                                                        scalar2=None, op0=Alu.bitwise_and))
                                v.wait_ge(cDV.sem, cDV.n)
                                cDV.inc(v.tensor_scalar(out=dq_t32[:], in0=dq_t32[:],
                                                        scalar1=float(2.0 ** (4 * sl4)),
                                                        scalar2=None, op0=Alu.mult))
                                v.wait_ge(cDV.sem, cDV.n)
                                cDV.inc(v.tensor_tensor(out=dq_acc[:], in0=dq_acc[:], in1=dq_t32[:], op=Alu.add))
                            v.wait_ge(cDV.sem, cDV.n)
                    steps.append(mux_slice)
                def wh_fin(ch=ch, q=q):
                    @blk.vector
                    def _(v, stp=sDQS.n):
                        v.wait_ge(sDQS.sem, stp)
                        cDV.inc(v.tensor_scalar(out=dq_wh[:], in0=dq_acc[:], scalar1=32768.0,
                                                scalar2=float(2.0 ** -FXS), op0=Alu.subtract, op1=Alu.mult))
                    dvn = cDV.n
                    @blk.gpsimd
                    def _(g, ch=ch, q=q, need=dvn):
                        g.wait_ge(cDV.sem, need)
                        sDQS.inc(g.dma_start(out=AGI[ch * 128:(ch + 1) * 128, 4096 + q * HN:4096 + (q + 1) * HN],
                                             in_=dq_wh[:]), 16)
                steps.append(wh_fin)
    def dq_pump(k):
        st = dq_state["steps"]
        while k > 0 and dq_state["i"] < len(st):
            st[dq_state["i"]]()
            dq_state["i"] += 1
            k -= 1
    def dq_finish():
        dq_pump(len(dq_state["steps"]))
        if fake_ag:
            @blk.gpsimd
            def _(g, need=sDQS.n):
                g.wait_ge(sDQS.sem, need)
                for c8 in range(8):
                    sDQS.inc(g.dma_start(out=AGO[c8 * 512:(c8 + 1) * 512, :], in_=AGI[:]), 16)
                cAG.inc(g.sem_inc(cAG.sem, 0)) if False else None
            @blk.gpsimd
            def _(g, need=sDQS.n):
                g.wait_ge(sDQS.sem, need)
                cAG.inc(g.engine_nop()) if False else None
                cAG.n += 1
                g.sem_inc(cAG.sem, 1)
        else:
            @blk.gpsimd
            def _(g, need=sDQS.n):
                g.wait_ge(sDQS.sem, need)
                cAG.inc(g.collective_compute(
                    "AllGather", mybir.AluOpType.bypass,
                    replica_groups=[list(range(8))],
                    ins=[AGI[:]],
                    outs=[AGO[:]],
                ))

    def p2_attention():
        sqs_b = bst
        von = sOBUF.n
        bstn = sBST.n
        for kh in range(8):
            if "dq" in stages:
                dq_pump(2)
            @blk.sync
            def _(s, kh=kh, dv=cDV.n, se=cSE.n, mm=cMM.n, von=von, bstn=bstn):
                s.wait_ge(cDV.sem, dv)
                s.wait_ge(cMM.sem, mm)
                s.wait_ge(sOBUF.sem, von)
                s.wait_ge(sBST.sem, bstn)
                sKH.inc(s.dma_start(out=kh_t[:], in_=KT[kh * 128:(kh + 1) * 128, :]), 16)
                for tkc in range(16):
                    sVH.inc(s.dma_start(out=vh[:, tkc * 128:(tkc + 1) * 128],
                                        in_=VS[tkc * 128:(tkc + 1) * 128, kh * 128:(kh + 1) * 128]), 16)
            khn = sKH.n
            vhn = sVH.n
            @blk.scalar
            def _(sc, need=khn):
                sc.wait_ge(sKH.sem, need)
                cSE.inc(sc.activation(out=sqs_b[:, 0:TC], in_=kh_t[:], func=Act.Square))
            sen = cSE.n
            for q4 in range(4):
                @blk.tensor
                def _(t, q4=q4, need=cSE.n):
                    t.wait_ge(cSE.sem, need)
                    cMM.inc(t.matmul(PS0[:], onesb[:], sqs_b[:, q4 * 512:(q4 + 1) * 512],
                                     start=True, stop=True))
                mmn = cMM.n
                @blk.scalar
                def _(sc, q4=q4, need=mmn):
                    sc.wait_ge(cMM.sem, need)
                    cSE.inc(sc.copy(out=rowa[:, q4 * 512:(q4 + 1) * 512], in_=PS0[:]))
            sen4 = cSE.n
            @blk.vector
            def _(v, need=sen4):
                v.wait_ge(cSE.sem, need)
                cDV.inc(v.reduce_max(out=rowb[:, 0:1], in_=rowa[:, 0:TC], axis=mybir.AxisListType.X))
            mkn = cDV.n
            kmm_gate = [cMM.n]

            for hq in range(4):
                h = kh * 4 + hq
                @blk.sync
                def _(s, h=h, dv=cDV.n, mm=cMM.n):
                    s.wait_ge(cDV.sem, dv)
                    s.wait_ge(cMM.sem, mm)
                    sQH.inc(s.dma_start(out=qh[:], in_=QT[h * 128:(h + 1) * 128, :]), 16)
                qhn = sQH.n
                @blk.scalar
                def _(sc, need=qhn, mmg=kmm_gate[0]):
                    sc.wait_ge(sQH.sem, need)
                    sc.wait_ge(cMM.sem, mmg)
                    cSE.inc(sc.activation(out=sqs_b[:, 0:TO], in_=qh[:], func=Act.Square))
                sen2 = cSE.n
                for q2 in range(2):
                    @blk.tensor
                    def _(t, q2=q2, need=cSE.n):
                        t.wait_ge(cSE.sem, need)
                        cMM.inc(t.matmul(PS0[:], onesb[:], sqs_b[:, q2 * 512:(q2 + 1) * 512],
                                         start=True, stop=True))
                    mmn2 = cMM.n
                    @blk.scalar
                    def _(sc, q2=q2, need=mmn2):
                        sc.wait_ge(cMM.sem, need)
                        cSE.inc(sc.copy(out=rowa[:, 1024 + q2 * 512:1024 + (q2 + 1) * 512], in_=PS0[:]))
                kmm_gate[0] = cMM.n
                seq0 = cSE.n
                @blk.vector
                def _(v, need=seq0, needm=mkn):
                    v.wait_ge(cSE.sem, need)
                    v.wait_ge(cDV.sem, needm)
                    cDV.inc(v.tensor_scalar(out=rowa[:, 1024:1024 + TO], in0=rowa[:, 1024:1024 + TO],
                                            scalar1=rowb[:, 0:1], scalar2=None, op0=Alu.mult))
                dvq2 = cDV.n
                @blk.scalar
                def _(sc, need=dvq2):
                    sc.wait_ge(cDV.sem, need)
                    cSE.inc(sc.activation(out=negMf[:], in_=rowa[:, 1024:1024 + TO], func=Act.Sqrt))
                seq = cSE.n
                @blk.vector
                def _(v, need=seq):
                    v.wait_ge(cSE.sem, need)
                    cDV.inc(v.tensor_scalar(out=negMb[:], in0=negMf[:], scalar1=-1.01, scalar2=None,
                                            op0=Alu.mult))
                negn = cDV.n
                if "dq" in stages:
                    dq_pump(2)

                for tt in range(2):
                    ntk = 8 * tt + 8
                    zm = cMM.n
                    @blk.scalar
                    def _(sc, mmp=zm):
                        sc.wait_ge(cMM.sem, mmp)
                        cSE.inc(sc.memzero(at_[0][:, 0:384]))
                        cSE.inc(sc.memzero(at_[1][:, 0:384]))
                    expg = [cSE.n, cSE.n]
                    dv0 = cDV.n
                    for tkc in range(ntk - 1, -1, -1):
                        jj = max(0, (tkc >> 1) - 4 * tt)
                        noff = jj * 128
                        sp = PP[tkc % 2]
                        atb = at_[tkc % 2]
                        j = tkc >> 1
                        diag = (4 * tt <= j < 4 * tt + 4)
                        @blk.tensor
                        def _(t, sp=sp, tkc=tkc, tt=tt, noff=noff, neg=negn, qn=qhn, kn=khn,
                              se=expg[tkc % 2], diag=diag, sel=tkc & 1,
                              first=(tkc == ntk - 1), dv0=dv0):
                            t.wait_ge(cDV.sem, neg)
                            if first:
                                t.wait_ge(cDV.sem, dv0)
                            t.wait_ge(sQH.sem, qn)
                            t.wait_ge(sKH.sem, kn)
                            t.wait_ge(cSE.sem, se)
                            cMM.inc(t.matmul(sp[:, noff:512], onesr1[:],
                                             negMb[0:1, tt * 512 + noff:(tt + 1) * 512],
                                             start=True, stop=False))
                            if diag:
                                cMM.inc(t.matmul(sp[:, noff:noff + 128], identb[:],
                                                 masktb[:, sel * 128:(sel + 1) * 128],
                                                 start=False, stop=False))
                            cMM.inc(t.matmul(sp[:, noff:512], kh_t[:, tkc * 128:(tkc + 1) * 128],
                                             qh[:, tt * 512 + noff:(tt + 1) * 512],
                                             start=False, stop=True))
                        mmn3 = cMM.n
                        @blk.scalar
                        def _(sc, sp=sp, atb=atb, noff=noff, need=mmn3):
                            sc.wait_ge(cMM.sem, need)
                            cSE.inc(sc.activation(out=atb[:, noff:512], in_=sp[:, noff:512],
                                                  func=Act.Exp, scale=SC))
                        expg[tkc % 2] = cSE.n
                        expn = cSE.n
                        @blk.tensor
                        def _(t, atb=atb, tkc=tkc, ntk=ntk, need=expn, vn=vhn):
                            t.wait_ge(cSE.sem, need)
                            t.wait_ge(sVH.sem, vn)
                            cMM.inc(t.matmul(PP[2][:], vh[:, tkc * 128:(tkc + 1) * 128], atb[:],
                                             start=(tkc == ntk - 1), stop=(tkc == 0)))
                            cMM.inc(t.matmul(PS1[:], onesb[:], atb[:],
                                             start=(tkc == ntk - 1), stop=(tkc == 0)))
                        if "dq" in stages and tkc % 4 == 2:
                            dq_pump(1)
                    avn = cMM.n
                    @blk.scalar
                    def _(sc, need=avn, dv=cDV.n):
                        sc.wait_ge(cMM.sem, need)
                        sc.wait_ge(cDV.sem, dv)
                        cSE.inc(sc.copy(out=rrow[:], in_=PS1[:]))
                    sen3 = cSE.n
                    @blk.vector
                    def _(v, need=sen3):
                        v.wait_ge(cSE.sem, need)
                        cDV.inc(v.reciprocal(out=rrow[:], in_=rrow[:]))
                    rcn = cDV.n
                    @blk.tensor
                    def _(t, need=rcn):
                        t.wait_ge(cDV.sem, need)
                        cMM.inc(t.matmul(PP[4][:], onesr1F[:], rrow[:], start=True, stop=True))
                    rbn = cMM.n
                    @blk.scalar
                    def _(sc, need=rbn, dv=cDV.n):
                        sc.wait_ge(cMM.sem, need)
                        sc.wait_ge(cDV.sem, dv)
                        cSE.inc(sc.copy(out=rtile[:], in_=PP[4][:]))
                    rbe = cSE.n
                    @blk.vector
                    def _(v, need=rbe, ob=sOB.n):
                        v.wait_ge(cSE.sem, need)
                        v.wait_ge(sOB.sem, ob)
                        cDV.inc(v.tensor_mul(out=obuf[:], in0=PP[2][:], in1=rtile[:]))
                    obn = cDV.n
                    @blk.scalar
                    def _(sc, h=h, tt=tt, need=obn):
                        sc.wait_ge(cDV.sem, need)
                        sOB.inc(sc.dma_start(out=OBT[h * 128:(h + 1) * 128, tt * 512:(tt + 1) * 512],
                                             in_=obuf[:]), 16)
                    if "dq" in stages:
                        dq_pump(1)

    def p2b_wo():
        obn_all = sOB.n
        @blk.sync
        def _(s, need=obn_all):
            s.wait_ge(sOB.sem, need)
        @blk.vector
        def _(v):
            cDV.inc(v.memset(s1row[:], 0.0))
            cDV.inc(v.memset(s2row[:], 0.0))
        i0 = 0
        while i0 < 32:
            grp = list(range(i0, min(i0 + 3, 32)))
            i0 += 3
            gw = len(grp)
            se_gate = cSE.n
            dv_gate = cDV.n
            for dh in range(32):
                pb = dh % 2
                @blk.sync
                def _(s, dh=dh, pb=pb, grp=grp, gw=gw, mg=guard["mm"][pb]):
                    s.wait_ge(cMM.sem, mg)
                    sOL[pb].inc(s.dma_start(out=x1r[pb][:], in_=OBT[dh * 128:(dh + 1) * 128, :]), 16)
                    sWF[pb].inc(s.dma_start(out=wf[pb][:, 0:gw * 128],
                                            in_=WO[dh * 128:(dh + 1) * 128, grp[0] * 128:(grp[0] + gw) * 128]), 16)
                ldw_n = sWF[pb].n
                ldx_n = sOL[pb].n
                @blk.vector
                def _(v, pb=pb, gw=gw, need=ldw_n, wg=guard["wcast"][pb]):
                    v.wait_ge(sWF[pb].sem, need)
                    v.wait_ge(cMM.sem, wg)
                    cDV.inc(v.tensor_copy(out=wb[pb][:, 0:gw * 128], in_=wf[pb][:, 0:gw * 128]))
                dvn = cDV.n
                @blk.tensor
                def _(t, dh=dh, pb=pb, gw=gw, need=dvn, needx=ldx_n, seg=se_gate, dvg=dv_gate):
                    t.wait_ge(cDV.sem, need)
                    t.wait_ge(sOL[pb].sem, needx)
                    if dh == 0:
                        t.wait_ge(cSE.sem, seg)
                        t.wait_ge(cDV.sem, dvg)
                    for mi in range(gw):
                        for tt2 in range(2):
                            cMM.inc(t.matmul(PP[mi * 2 + tt2][:], wb[pb][:, mi * 128:(mi + 1) * 128],
                                             x1r[pb][:, tt2 * 512:(tt2 + 1) * 512],
                                             start=(dh == 0), stop=(dh == 31)))
                guard["mm"][pb] = cMM.n
                guard["wcast"][pb] = cMM.n
            mm_done = cMM.n
            for mi, m in enumerate(grp):
                for tt2 in range(2):
                    ptile = PP[mi * 2 + tt2]
                    @blk.sync
                    def _(s, m=m, tt2=tt2, dv=cDV.n):
                        s.wait_ge(cDV.sem, dv)
                        sRT.inc(s.dma_start(out=rtile2[:], in_=XTO[m * 128:(m + 1) * 128,
                                                                  tt2 * 512:(tt2 + 1) * 512]), 16)
                    rtn = sRT.n
                    @blk.vector
                    def _(v, ptile=ptile, need=mm_done, nrt=rtn, dvp=cDV.n, stp=sX1.n):
                        v.wait_ge(cMM.sem, need)
                        v.wait_ge(sRT.sem, nrt)
                        v.wait_ge(sX1.sem, stp)
                        cDV.inc(v.tensor_add(out=rtile[:], in0=ptile[:], in1=rtile2[:]))
                    dvr = cDV.n
                    @blk.scalar
                    def _(sc, m=m, tt2=tt2, need=dvr):
                        sc.wait_ge(cDV.sem, need)
                        sX1.inc(sc.dma_start(out=R1[m * 128:(m + 1) * 128, tt2 * 512:(tt2 + 1) * 512],
                                             in_=rtile[:]), 16)
                        cSE.inc(sc.activation(out=rtile2[:], in_=rtile[:], func=Act.Square))
                    sqn = cSE.n
                    @blk.tensor
                    def _(t, need=dvr, nsq=sqn, dvp=cDV.n):
                        t.wait_ge(cDV.sem, dvp)
                        t.wait_ge(cDV.sem, need)
                        cMM.inc(t.matmul(PS0[:], onesbF[:], rtile[:], start=True, stop=True))
                        t.wait_ge(cSE.sem, nsq)
                        cMM.inc(t.matmul(PS1[:], onesbF[:], rtile2[:], start=True, stop=True))
                    smn = cMM.n
                    @blk.vector
                    def _(v, tt2=tt2, need=smn):
                        v.wait_ge(cMM.sem, need)
                        cDV.inc(v.tensor_add(out=s1row[0:1, tt2 * 512:(tt2 + 1) * 512],
                                             in0=s1row[0:1, tt2 * 512:(tt2 + 1) * 512], in1=PS0[:]))
                        cDV.inc(v.tensor_add(out=s2row[0:1, tt2 * 512:(tt2 + 1) * 512],
                                             in0=s2row[0:1, tt2 * 512:(tt2 + 1) * 512], in1=PS1[:]))

        @blk.vector
        def _(v):
            v.wait_ge(cDV.sem, cDV.n)
            cDV.inc(v.tensor_scalar(out=mrow[:], in0=s1row[:], scalar1=float(1.0 / 4096.0), scalar2=None,
                                    op0=Alu.mult))
            v.wait_ge(cDV.sem, cDV.n)
            cDV.inc(v.tensor_mul(out=rowa[:, 0:TO], in0=mrow[:], in1=mrow[:]))
            cDV.inc(v.tensor_scalar(out=rowb[:, 0:TO], in0=s2row[:], scalar1=float(1.0 / 4096.0),
                                    scalar2=None, op0=Alu.mult))
            v.wait_ge(cDV.sem, cDV.n)
            cDV.inc(v.tensor_sub(out=rowb[:, 0:TO], in0=rowb[:, 0:TO], in1=rowa[:, 0:TO]))
        vvn = cDV.n
        @blk.vector
        def _(v, need=vvn):
            v.wait_ge(cDV.sem, need)
            cDV.inc(v.tensor_scalar(out=rowb[0:1, 0:TO], in0=rowb[0:1, 0:TO], scalar1=float(EPS),
                                    scalar2=None, op0=Alu.add))
        vvn2 = cDV.n
        @blk.scalar
        def _(sc, need=vvn2):
            sc.wait_ge(cDV.sem, need)
            cSE.inc(sc.activation(out=rowb[0:1, 0:TO], in_=rowb[0:1, 0:TO], func=Act.Sqrt))
        ssn = cSE.n
        @blk.vector
        def _(v, need=ssn):
            v.wait_ge(cSE.sem, need)
            cDV.inc(v.reciprocal(out=rstd[:], in_=rowb[0:1, 0:TO]))
            v.wait_ge(cDV.sem, cDV.n)
            cDV.inc(v.tensor_scalar(out=rowa[:, 0:TO], in0=rowa[:, 0:TO], scalar1=4096.0, scalar2=None,
                                    op0=Alu.mult))
            v.wait_ge(cDV.sem, cDV.n)
            cDV.inc(v.tensor_sub(out=rowa[:, 0:TO], in0=s2row[:], in1=rowa[:, 0:TO]))
            cDV.inc(v.tensor_mul(out=rowb[0:1, 0:TO], in0=rstd[:], in1=rstd[:]))
            v.wait_ge(cDV.sem, cDV.n)
            cDV.inc(v.tensor_mul(out=rowa[:, 0:TO], in0=rowa[:, 0:TO], in1=rowb[0:1, 0:TO]))
        een = cDV.n
        @blk.scalar
        def _(sc, need=een):
            sc.wait_ge(cDV.sem, need)
            cSE.inc(sc.activation(out=erow[:], in_=rowa[0:1, 0:TO], func=Act.Sqrt))
        esn = cSE.n
        @blk.vector
        def _(v, need=esn):
            v.wait_ge(cSE.sem, need)
            cDV.inc(v.tensor_scalar(out=mskrow[:], in0=erow[:], scalar1=64.0, scalar2=None, op0=Alu.is_gt))
        mbn = cDV.n
        for q2 in range(2):
            @blk.tensor
            def _(t, q2=q2, need=mbn, dv=cDV.n):
                t.wait_ge(cDV.sem, max(need, dv))
                cMM.inc(t.matmul(PP[4][:], onesr1F[:], mskrow[0:1, q2 * 512:(q2 + 1) * 512],
                                 start=True, stop=True))
            mqn = cMM.n
            @blk.vector
            def _(v, q2=q2, need=mqn):
                v.wait_ge(cMM.sem, need)
                cDV.inc(v.tensor_copy(out=maskbc[:, q2 * 512:(q2 + 1) * 512], in_=PP[4][:]))

    def p2c_ln1():
        for tt2 in range(2):
            @blk.tensor
            def _(t, tt2=tt2, need=cDV.n):
                t.wait_ge(cDV.sem, need)
                cMM.inc(t.matmul(PP[4][:], onesr1F[:], mrow[0:1, tt2 * 512:(tt2 + 1) * 512],
                                 start=True, stop=True))
                cMM.inc(t.matmul(PP[5][:], onesr1F[:], rstd[0:1, tt2 * 512:(tt2 + 1) * 512],
                                 start=True, stop=True))
            bcn = cMM.n
            for do in range(32):
                @blk.sync
                def _(s, do=do, tt2=tt2, dv=cDV.n, stp=sX1.n):
                    s.wait_ge(cDV.sem, dv)
                    s.wait_ge(sX1.sem, stp)
                    sRT.inc(s.dma_start(out=rtile[:], in_=R1[do * 128:(do + 1) * 128,
                                                           tt2 * 512:(tt2 + 1) * 512]), 16)
                rtn = sRT.n
                @blk.vector
                def _(v, need=rtn, bc=bcn, stp=sX1.n):
                    v.wait_ge(sRT.sem, need)
                    v.wait_ge(cMM.sem, bc)
                    v.wait_ge(sX1.sem, stp)
                    cDV.inc(v.tensor_sub(out=rtile[:], in0=rtile[:], in1=PP[4][:]))
                    v.wait_ge(cDV.sem, cDV.n)
                    cDV.inc(v.tensor_mul(out=rtile[:], in0=rtile[:], in1=PP[5][:]))
                    v.wait_ge(cDV.sem, cDV.n)
                    cDV.inc(v.tensor_copy(out=obuf[:], in_=rtile[:]))
                dvn = cDV.n
                @blk.scalar
                def _(sc, do=do, tt2=tt2, need=dvn):
                    sc.wait_ge(cDV.sem, need)
                    sX1.inc(sc.dma_start(out=X1F[do * 128:(do + 1) * 128, tt2 * 512:(tt2 + 1) * 512],
                                         in_=rtile[:]), 16)
                    sX1.inc(sc.dma_start(out=X1B[do * 128:(do + 1) * 128, tt2 * 512:(tt2 + 1) * 512],
                                         in_=obuf[:]), 16)

    if "dq" in stages:
        dq_make_steps()
    if "p1" in stages:
        @blk.sync
        def _(s):
            s.wait_ge(cGD.sem, n_cast)
        proj_std(32, TO, WQ, XTOb, QT, cosO, sinO, 3)
        proj_std(8, TC, WK, XTb, KT, cosC, sinC, 1)
        proj_v()
    kh_t = kh
    if "p2" in stages:
        p2_attention()
    if "dq" in stages:
        dq_finish()
    es_dq.close()

    def p3_ffn():
        x1res = es.enter_context(nc.sbuf_tensor("x1res", [128, 32 * 1024], BF16))
        ylt = [es.enter_context(nc.sbuf_tensor(f"ylt{i}", [128, 512], F32)) for i in range(6)]
        wtt = [es.enter_context(nc.sbuf_tensor(f"wtt{i}", [128, 384], BF16)) for i in range(2)]
        @blk.sync
        def _(s, agn=cAG.n, x1n=sX1.n):
            s.wait_ge(cAG.sem, agn)
            s.wait_ge(sX1.sem, x1n)
            for ic in range(32):
                sXR.inc(s.dma_start(out=x1res[:, ic * 1024:(ic + 1) * 1024],
                                    in_=X1B[ic * 128:(ic + 1) * 128, :]), 16)
        xrn = sXR.n
        @blk.vector
        def _(v, dv=cDV.n):
            v.wait_ge(cDV.sem, dv)
            cDV.inc(v.memset(s1row[:], 0.0))
            cDV.inc(v.memset(s2row[:], 0.0))
        i0 = 0
        while i0 < 32:
            grp = list(range(i0, min(i0 + 3, 32)))
            i0 += 3
            gw = len(grp)
            ylts = {}
            for tbl in range(2):
                se_gate = cSE.n
                dv_gate = cDV.n
                for ic in range(32):
                    pb = ic % 2
                    @blk.sync
                    def _(s, ic=ic, pb=pb, grp=grp, gw=gw, tbl=tbl, mg=guard["mm"][pb], xn=xrn):
                        s.wait_ge(cMM.sem, mg)
                        s.wait_ge(sXR.sem, xn)
                        sWF[pb].inc(s.dma_start(
                            out=wtt[pb][:, 0:gw * 128],
                            in_=AGO[grp[0] * 128:(grp[0] + gw) * 128,
                                    tbl * 4096 + ic * 128:tbl * 4096 + (ic + 1) * 128],
                            transpose=True), 16)
                    ldw_n = sWF[pb].n
                    @blk.tensor
                    def _(t, ic=ic, pb=pb, gw=gw, need=ldw_n, seg=se_gate, dvg=dv_gate, tbl=tbl):
                        t.wait_ge(sWF[pb].sem, need)
                        if ic == 0:
                            t.wait_ge(cSE.sem, seg)
                            t.wait_ge(cDV.sem, dvg)
                        for mi in range(gw):
                            for tt2 in range(2):
                                cMM.inc(t.matmul(PP[mi * 2 + tt2][:], wtt[pb][:, mi * 128:(mi + 1) * 128],
                                                 x1res[:, ic * 1024 + tt2 * 512:ic * 1024 + (tt2 + 1) * 512],
                                                 start=(ic == 0), stop=(ic == 31)))
                    guard["mm"][pb] = cMM.n
                mm_done = cMM.n
                if tbl == 0:
                    for mi in range(gw):
                        for tt2 in range(2):
                            @blk.scalar
                            def _(sc, k=mi * 2 + tt2, need=mm_done, dv=cDV.n):
                                sc.wait_ge(cMM.sem, need)
                                sc.wait_ge(cDV.sem, dv)
                                cSE.inc(sc.copy(out=ylt[k][:], in_=PP[k][:]))
                else:
                    for mi, m in enumerate(grp):
                        for tt2 in range(2):
                            k = mi * 2 + tt2
                            @blk.sync
                            def _(s, m=m, tt2=tt2, dv=cDV.n, stp=sX1.n):
                                s.wait_ge(cDV.sem, dv)
                                s.wait_ge(sX1.sem, stp)
                                sRT.inc(s.dma_start(out=rtile2[:], in_=X1F[m * 128:(m + 1) * 128,
                                                    tt2 * 512:(tt2 + 1) * 512]), 16)
                            rtn = sRT.n
                            @blk.vector
                            def _(v, k=k, tt2=tt2, need=mm_done, se=cSE.n, stp=sX1.n):
                                v.wait_ge(cMM.sem, need)
                                v.wait_ge(cSE.sem, se)
                                v.wait_ge(sX1.sem, stp)
                                cDV.inc(v.tensor_sub(out=rtile[:], in0=PP[k][:], in1=ylt[k][:]))
                                v.wait_ge(cDV.sem, cDV.n)
                                cDV.inc(v.tensor_mul(out=rtile[:], in0=rtile[:],
                                                     in1=maskbc[:, tt2 * 512:(tt2 + 1) * 512]))
                                v.wait_ge(cDV.sem, cDV.n)
                                cDV.inc(v.tensor_add(out=rtile[:], in0=rtile[:], in1=ylt[k][:]))
                            dvy = cDV.n
                            @blk.vector
                            def _(v, m=m, need=dvy, nrt=rtn):
                                v.wait_ge(cDV.sem, need)
                                v.wait_ge(sRT.sem, nrt)
                                cDV.inc(v.tensor_scalar(out=rtile[:], in0=rtile[:], scalar1=biask[:, m:m + 1],
                                                        scalar2=None, op0=Alu.add))
                                v.wait_ge(cDV.sem, cDV.n)
                                cDV.inc(v.tensor_add(out=rtile[:], in0=rtile[:], in1=rtile2[:]))
                            dvr = cDV.n
                            @blk.scalar
                            def _(sc, m=m, tt2=tt2, need=dvr):
                                sc.wait_ge(cDV.sem, need)
                                sX1.inc(sc.dma_start(out=R2[m * 128:(m + 1) * 128,
                                                            tt2 * 512:(tt2 + 1) * 512], in_=rtile[:]), 16)
                                cSE.inc(sc.activation(out=rtile2[:], in_=rtile[:], func=Act.Square))
                            sqn = cSE.n
                            @blk.tensor
                            def _(t, need=dvr, nsq=sqn, dvp=cDV.n):
                                t.wait_ge(cDV.sem, dvp)
                                cMM.inc(t.matmul(PS0[:], onesbF[:], rtile[:], start=True, stop=True))
                                t.wait_ge(cSE.sem, nsq)
                                cMM.inc(t.matmul(PS1[:], onesbF[:], rtile2[:], start=True, stop=True))
                            smn = cMM.n
                            @blk.vector
                            def _(v, tt2=tt2, need=smn):
                                v.wait_ge(cMM.sem, need)
                                cDV.inc(v.tensor_add(out=s1row[0:1, tt2 * 512:(tt2 + 1) * 512],
                                                     in0=s1row[0:1, tt2 * 512:(tt2 + 1) * 512], in1=PS0[:]))
                                cDV.inc(v.tensor_add(out=s2row[0:1, tt2 * 512:(tt2 + 1) * 512],
                                                     in0=s2row[0:1, tt2 * 512:(tt2 + 1) * 512], in1=PS1[:]))
        @blk.vector
        def _(v):
            v.wait_ge(cDV.sem, cDV.n)
            cDV.inc(v.tensor_scalar(out=m2row[:], in0=s1row[:], scalar1=float(1.0 / 4096.0), scalar2=None,
                                    op0=Alu.mult))
            v.wait_ge(cDV.sem, cDV.n)
            cDV.inc(v.tensor_mul(out=rowa[:, 0:TO], in0=m2row[:], in1=m2row[:]))
            cDV.inc(v.tensor_scalar(out=rowb[:, 0:TO], in0=s2row[:], scalar1=float(1.0 / 4096.0),
                                    scalar2=None, op0=Alu.mult))
            v.wait_ge(cDV.sem, cDV.n)
            cDV.inc(v.tensor_sub(out=rowb[:, 0:TO], in0=rowb[:, 0:TO], in1=rowa[:, 0:TO]))
            v.wait_ge(cDV.sem, cDV.n)
            cDV.inc(v.tensor_scalar(out=rowb[0:1, 0:TO], in0=rowb[0:1, 0:TO], scalar1=float(EPS),
                                    scalar2=None, op0=Alu.add))
        vvn = cDV.n
        @blk.scalar
        def _(sc, need=vvn):
            sc.wait_ge(cDV.sem, need)
            cSE.inc(sc.activation(out=rowb[0:1, 0:TO], in_=rowb[0:1, 0:TO], func=Act.Sqrt))
        ssn = cSE.n
        @blk.vector
        def _(v, need=ssn):
            v.wait_ge(cSE.sem, need)
            cDV.inc(v.reciprocal(out=r2row[:], in_=rowb[0:1, 0:TO]))

    def p4_out():
        for tt2 in range(2):
            @blk.tensor
            def _(t, tt2=tt2, need=cDV.n, ses=cSE.n):
                t.wait_ge(cDV.sem, need)
                t.wait_ge(cSE.sem, ses)
                cMM.inc(t.matmul(PP[4][:], onesr1F[:], m2row[0:1, tt2 * 512:(tt2 + 1) * 512],
                                 start=True, stop=True))
                cMM.inc(t.matmul(PP[5][:], onesr1F[:], r2row[0:1, tt2 * 512:(tt2 + 1) * 512],
                                 start=True, stop=True))
            bcn = cMM.n
            for do in range(32):
                @blk.sync
                def _(s, do=do, tt2=tt2, dv=cDV.n, stp=sOUT.n, sx=sX1.n):
                    s.wait_ge(cDV.sem, dv)
                    s.wait_ge(sOUT.sem, stp)
                    s.wait_ge(sX1.sem, sx)
                    sRT.inc(s.dma_start(out=rtile[:], in_=R2[do * 128:(do + 1) * 128,
                                                           tt2 * 512:(tt2 + 1) * 512]), 16)
                rtn = sRT.n
                @blk.vector
                def _(v, need=rtn, bc=bcn):
                    v.wait_ge(sRT.sem, need)
                    v.wait_ge(cMM.sem, bc)
                    cDV.inc(v.tensor_sub(out=rtile[:], in0=rtile[:], in1=PP[4][:]))
                    v.wait_ge(cDV.sem, cDV.n)
                    cDV.inc(v.tensor_mul(out=rtile[:], in0=rtile[:], in1=PP[5][:]))
                dvn = cDV.n
                @blk.scalar
                def _(sc, do=do, tt2=tt2, need=dvn):
                    sc.wait_ge(cDV.sem, need)
                    sOUT.inc(sc.dma_start(out=OUT[do * 128:(do + 1) * 128, tt2 * 512:(tt2 + 1) * 512],
                                          in_=rtile[:]), 16)

    if "p2b" in stages:
        p2b_wo()
    if "p2c" in stages:
        p2c_ln1()
    if "p3" in stages:
        p3_ffn()
    if "p4" in stages:
        p4_out()

    if dbg:
        @blk.sync
        def _(s):
            for c in ALL_CTRS:
                if c.n:
                    s.wait_ge(c.sem, c.n)
            for name, dst in dbg_map.items():
                src = dram[name]
                rows = src.shape[0]
                step = max(1, (1 << 20) // int(np.prod(src.shape[1:]) * 4))
                r = 0
                while r < rows:
                    rr = min(step, rows - r)
                    cLD.inc(s.dma_start(out=dst[r:r + rr], in_=src[r:r + rr]), 16)
                    r += rr
            s.wait_ge(cLD.sem, cLD.n)

    @blk.sync
    def _(s):
        for c in ALL_CTRS:
            if c.n:
                s.wait_ge(c.sem, c.n)

    es.close()
    return nc, dram


def rope_tables_np():
    import jax
    with jax.default_device(jax.devices("cpu")[0]):
        import jax.numpy as jnp
        half = 64
        inv_freq = 1.0 / 10000.0 ** (jnp.arange(0, half, dtype=jnp.float32) / half)
        pos = jnp.arange(2048, dtype=jnp.float32)
        ang = pos[:, None] * inv_freq[None, :]
        cos = np.asarray(jnp.cos(ang))
        sin = np.asarray(jnp.sin(ang))
    cosfm = np.ascontiguousarray(np.concatenate([cos.T, cos.T], axis=0).astype(np.float32))
    sinfm = np.ascontiguousarray(np.concatenate([-sin.T, sin.T], axis=0).astype(np.float32))
    return cosfm, sinfm


def host_inputs(core, X, Wq, Wk, Wv, Wo, qweight, lut4, lut8, bias):
    b, half = core >> 1, core & 1
    cosfm, sinfm = rope_tables_np()
    own_cols = np.concatenate([np.arange((2 * j + half) * 128, (2 * j + half) * 128 + 128) for j in range(8)])
    XTc = np.ascontiguousarray(X[b].T.astype(np.float32))
    XTOc = np.ascontiguousarray(XTc[:, own_cols])
    tk = np.arange(128)[:, None]
    tq = np.arange(128)[None, :]
    if half == 0:
        m0 = np.where(tk <= tq, 0.0, -1e30)
        m1 = np.full((128, 128), -1e30)
    else:
        m0 = np.zeros((128, 128))
        m1 = np.where(tk <= tq, 0.0, -1e30)
    masks = np.ascontiguousarray(np.stack([m0, m1]).astype(np.float32))
    qws = np.ascontiguousarray(
        qweight[:, 512 * core:512 * (core + 1), :].reshape(8, 4, 128, 128).transpose(0, 2, 1, 3))
    lut4s = np.ascontiguousarray(lut4[512 * core:512 * (core + 1)].reshape(4, 128, 16).transpose(1, 0, 2))
    lut8s = np.ascontiguousarray(lut8[512 * core:512 * (core + 1)].reshape(4, 128, 256).transpose(1, 0, 2))
    biasr = np.ascontiguousarray(bias.reshape(32, 128).T.astype(np.float32))
    return {
        "XT": XTc, "XTO": XTOc,
        "WQ": np.ascontiguousarray(Wq.astype(np.float32)),
        "WK": np.ascontiguousarray(Wk.astype(np.float32)),
        "WV": np.ascontiguousarray(Wv.astype(np.float32)),
        "WO": np.ascontiguousarray(Wo.astype(np.float32)),
        "QWS": qws, "LUT4S": lut4s, "LUT8S": lut8s, "BIASR": biasr,
        "COSO": np.ascontiguousarray(cosfm[:, own_cols]), "SINO": np.ascontiguousarray(sinfm[:, own_cols]),
        "COSC": cosfm, "SINC": sinfm,
        "MASKB": masks.astype(ml_dtypes.bfloat16),
        "IDENT": np.eye(128, dtype=ml_dtypes.bfloat16),
    }



from concourse.bass_utils import run_bass_kernel_spmd

_BUILD_CACHE = {}

def kernel(L0_attn_X, L0_attn_Wq, L0_attn_Wk, L0_attn_Wv, L0_attn_Wo,
           L0_ffn_qweight, L0_ffn_lut4, L0_ffn_lut8, L0_ffn_bias):
    X = np.asarray(L0_attn_X)
    Wq = np.asarray(L0_attn_Wq); Wk = np.asarray(L0_attn_Wk)
    Wv = np.asarray(L0_attn_Wv); Wo = np.asarray(L0_attn_Wo)
    qw = np.asarray(L0_ffn_qweight)
    lut4 = np.asarray(L0_ffn_lut4); lut8 = np.asarray(L0_ffn_lut8)
    bias = np.asarray(L0_ffn_bias)

    nc, dram = build(stages=("p1", "dq", "p2", "p2b", "p2c", "p3", "p4"))
    in_maps = [host_inputs(c, X, Wq, Wk, Wv, Wo, qw, lut4, lut8, bias) for c in range(8)]
    res = run_bass_kernel_spmd(nc, in_maps, core_ids=list(range(8)))
    out = np.empty((4, 2048, 4096), dtype=np.float32)
    for c in range(8):
        b, half = c >> 1, c & 1
        oc = np.asarray(res.results[c]["OUT"])
        for j in range(8):
            out[b, (2 * j + half) * 128:(2 * j + half) * 128 + 128, :] = \
                oc[:, j * 128:(j + 1) * 128].T
    return out


# version 15
# speedup vs baseline: 1.1577x; 1.0950x over previous
import contextlib
import numpy as np
import ml_dtypes
import concourse.bass as bass
import concourse.mybir as mybir

F32 = mybir.dt.float32
BF16 = mybir.dt.bfloat16
I32 = mybir.dt.int32
U16 = mybir.dt.uint16
U8 = mybir.dt.uint8
Alu = mybir.AluOpType
Act = mybir.ActivationFunctionType

D = 4096
TC = 2048
TO = 1024
SC = float(1.0 / np.float64(np.float32(np.sqrt(np.float32(128.0)))))
EPS = 1e-5
HN = 1024

class Ctr:
    def __init__(self, es, nc, name):
        self.sem = es.enter_context(nc.semaphore(name))
        self.n = 0
    def inc(self, instr, k=1):
        self.n += k
        instr.then_inc(self.sem, k)
        return instr


def build(stages, dbg=(), fake_ag=False):
    nc = bass.Bass("TRN2", target_bir_lowering=False, debug=False)
    es = contextlib.ExitStack()
    dram = {}

    def din(name, shape, dt):
        dram[name] = nc.dram_tensor(name, shape, dt, kind="ExternalInput").ap(); return dram[name]
    def dout(name, shape, dt):
        dram[name] = nc.dram_tensor(name, shape, dt, kind="ExternalOutput").ap(); return dram[name]
    def dint(name, shape, dt):
        dram[name] = nc.dram_tensor(name, shape, dt, kind="Internal").ap(); return dram[name]
    def sb(name, shape, dt):
        return es.enter_context(nc.sbuf_tensor(name, shape, dt))
    def psum(name, shape):
        return es.enter_context(nc.psum_tensor(name, shape, F32))

    XT = din("XT", [D, TC], F32)
    XTO = din("XTO", [D, TO], F32)
    WQ = din("WQ", [D, D], F32)
    WK = din("WK", [D, 1024], F32)
    WV = din("WV", [D, 1024], F32)
    WO = din("WO", [D, D], F32)
    QWS = din("QWS", [8, 128, 4, 128], I32)
    LUT4S = din("LUT4S", [128, 4, 16], F32)
    LUT8S = din("LUT8S", [128, 4, 256], F32)
    BIASR = din("BIASR", [128, 32], F32)
    COSO = din("COSO", [128, TO], F32)
    SINO = din("SINO", [128, TO], F32)
    COSC = din("COSC", [128, TC], F32)
    SINC = din("SINC", [128, TC], F32)
    MASKB = din("MASKB", [2, 128, 128], BF16)
    IDENT = din("IDENT", [128, 128], BF16)
    OUT = dout("OUT", [D, TO], F32)

    XTb = dint("XTb", [D, TC], BF16)
    XTOb = dint("XTOb", [D, TO], BF16)
    QT = dint("QTs", [D, TO], BF16)
    KT = dint("KTs", [1024, TC], BF16)
    VS = dint("VSs", [TC, 1024], BF16)
    OBT = dint("OBT", [D, TO], BF16)
    R1 = dint("R1", [D, TO], F32)
    X1F = dint("X1F", [D, TO], F32)
    X1B = dint("X1B", [D, TO], BF16)
    R2 = dint("R2", [D, TO], F32)
    AGI = dint("AGI", [512, 8192], BF16)
    AGO = dint("AGO", [D, 8192], BF16)

    dbg_map = {}
    for name in dbg:
        src = dram[name]
        dbg_map[name] = dout("DBG_" + name, list(src.shape), src.dtype)

    PP = [psum(f"pp{i}", [128, 512]) for i in range(6)]
    PS0 = psum("ps0", [1, 512])
    PS1 = psum("ps1", [1, 512])

    wf = [sb(f"wf{i}", [128, 512], F32) for i in range(2)]
    wb = [sb(f"wb{i}", [128, 512], BF16) for i in range(2)]
    xb = [sb(f"xb{i}", [128, TC], BF16) for i in range(2)]
    qraw = sb("qraw", [128, TC], F32)
    qswp = sb("qswp", [128, TC], F32)
    bst = sb("bst", [128, TC], BF16)
    cosO = sb("cosO", [128, TO], F32)
    sinO = sb("sinO", [128, TO], F32)
    cosC = sb("cosC", [128, TC], F32)
    sinC = sb("sinC", [128, TC], F32)
    masktb = sb("masktb", [128, 256], BF16)
    identb = sb("identb", [128, 128], BF16)
    biask = sb("biask", [128, 32], F32)

    qh = sb("qh", [128, TO], BF16)
    kh = sb("kh", [128, TC], BF16)
    vh = sb("vh", [128, TC], BF16)
    at_ = [sb(f"at{i}", [128, 512], BF16) for i in range(2)]
    sqs = qswp
    onesb = sb("onesb", [128, 1], BF16)
    onesr1 = sb("onesr1", [1, 128], BF16)
    onesr1F = sb("onesr1F", [1, 128], F32)
    rowsA = sb("rowsA", [1, 5376], F32)
    rowa = rowsA[0:1, 0:TC]
    rowb = rowsA[0:1, TC:TC + 1280]
    negMf = rowsA[0:1, 3328:4352]
    negMb = sb("negMb", [1, TO], BF16)
    rrow = sb("rrow", [1, 512], F32)
    obuf = sb("obuf", [128, 512], BF16)
    s1row = sb("s1row", [1, TO], F32)
    s2row = sb("s2row", [1, TO], F32)
    mrow = sb("mrow", [1, TO], F32)
    rstd = sb("rstd", [1, TO], F32)
    mskrow = sb("mskrow", [1, TO], F32)
    erow = rowsA[0:1, 4352:5376]
    m2row = mrow
    r2row = rstd
    maskbc = sb("maskbc", [128, TO], F32)
    rtile = sb("rtile", [128, 512], F32)
    rtile2 = sb("rtile2", [128, 512], F32)
    x1r = [sb(f"x1r{i}", [128, TO], BF16) for i in range(2)]
    onesbF = sb("onesbF", [128, 1], F32)

    es_dq = contextlib.ExitStack()
    def sbq(name, shape, dt):
        return es_dq.enter_context(nc.sbuf_tensor(name, shape, dt))
    dq_qw8 = [sbq(f"dq_qw8{i}", [128, 8 * 512], U8) for i in range(2)]
    dq_l4 = [sbq(f"dq_l4{i}", [128, 16], F32) for i in range(2)]
    dq_l8 = [sbq(f"dq_l8{i}", [128, 256], F32) for i in range(2)]
    dq_l4b = sbq("dq_l4b", [128, 16], BF16)
    dq_XW4 = sbq("dq_XW4", [128, 4], I32)
    dq_idx = sbq("dq_idx", [128, 4096], U16)
    dq_ci = sbq("dq_ci", [128, HN], I32)
    dq_t8 = sbq("dq_t8", [128, 512], U8)
    dq_wl = sbq("dq_wl", [128, HN], BF16)
    dq_wh = sbq("dq_wh", [128, HN], BF16)
    dq_V16 = sbq("dq_V16", [128, 256], I32)
    dq_W32 = sbq("dq_W32", [128, 4 * 32], I32)
    dq_XW = sbq("dq_XW", [128, 4 * 16], I32)
    dq_msk = sbq("dq_msk", [128, 5 * HN], I32)
    dq_mux = sbq("dq_mux", [128, 5 * HN], I32)
    dq_sham = sbq("dq_sham", [128, HN], I32)
    dq_acc = dq_ci
    dq_t32 = sbq("dq_t32", [128, HN], I32)

    cLD = Ctr(es, nc, "cLD")
    sCONST = Ctr(es, nc, "sCONST")
    sXB = [Ctr(es, nc, "sXB0"), Ctr(es, nc, "sXB1")]
    sWF = [Ctr(es, nc, "sWF0"), Ctr(es, nc, "sWF1")]
    sSW = Ctr(es, nc, "sSW")
    sBST = Ctr(es, nc, "sBST")
    sOBUF = Ctr(es, nc, "sOBUF")
    sQH = Ctr(es, nc, "sQH")
    sKH = Ctr(es, nc, "sKH")
    sVH = Ctr(es, nc, "sVH")
    sOB = Ctr(es, nc, "sOB")
    sOL = [Ctr(es, nc, "sOL0"), Ctr(es, nc, "sOL1")]
    sRT = Ctr(es, nc, "sRT")
    sX1 = Ctr(es, nc, "sX1")
    cST = Ctr(es, nc, "cST")
    cGD = Ctr(es, nc, "cGD")
    cMM = Ctr(es, nc, "cMM")
    cSE = Ctr(es, nc, "cSE")
    cDV = Ctr(es, nc, "cDV")
    cAG = Ctr(es, nc, "cAG")
    sDQ = Ctr(es, nc, "sDQ")
    sDQS = Ctr(es, nc, "sDQS")
    sXR = Ctr(es, nc, "sXR")
    sOUT = Ctr(es, nc, "sOUT")
    ALL_CTRS = [cLD, sCONST, sSW, sBST, sOBUF, sQH, sKH, sVH, sOB, sRT, sX1, cST, cGD, cMM, cSE, cDV, cAG, sDQ, sDQS, sXR, sOUT] + sXB + sWF + sOL

    blk = es.enter_context(nc.Block())

    @blk.sync
    def _(s):
        sCONST.inc(s.dma_start(out=cosO[:], in_=COSO[:]), 16)
        sCONST.inc(s.dma_start(out=sinO[:], in_=SINO[:]), 16)
        sCONST.inc(s.dma_start(out=cosC[:], in_=COSC[:]), 16)
        sCONST.inc(s.dma_start(out=sinC[:], in_=SINC[:]), 16)
        sCONST.inc(s.dma_start(out=masktb[:, 0:128], in_=MASKB[0]), 16)
        sCONST.inc(s.dma_start(out=masktb[:, 128:256], in_=MASKB[1]), 16)
        sCONST.inc(s.dma_start(out=identb[:], in_=IDENT[:]), 16)
        sCONST.inc(s.dma_start(out=biask[:], in_=BIASR[:]), 16)

    @blk.vector
    def _(v):
        cDV.inc(v.memset(onesb[:], 1.0))
        cDV.inc(v.memset(onesr1[:], 1.0))
        cDV.inc(v.memset(onesr1F[:], 1.0))
        cDV.inc(v.memset(onesbF[:], 1.0))
        v.wait_ge(sCONST.sem, sCONST.n)

    if "p1" in stages:
        @blk.gpsimd
        def _(g):
            for i in range(8):
                cGD.inc(g.dma_start(out=XTb[i * 512:(i + 1) * 512, :], in_=XT[i * 512:(i + 1) * 512, :]), 16)
            for i in range(4):
                cGD.inc(g.dma_start(out=XTOb[i * 1024:(i + 1) * 1024, :], in_=XTO[i * 1024:(i + 1) * 1024, :]), 16)
    n_cast = cGD.n

    guard = {"mm": [0, 0], "wcast": [0, 0]}

    def proj_std(MCH, TN, W, XSRC, OUTT, cost, sint, G):
        ntt = TN // 512
        i0 = 0
        while i0 < MCH:
            grp = list(range(i0, min(i0 + G, MCH)))
            i0 += G
            gw = len(grp)
            se_gate = cSE.n
            for ic in range(32):
                pb = ic % 2
                @blk.sync
                def _(s, ic=ic, pb=pb, grp=grp, gw=gw, mg=guard["mm"][pb], W=W, XSRC=XSRC, TN=TN):
                    s.wait_ge(cMM.sem, mg)
                    sXB[pb].inc(s.dma_start(out=xb[pb][:, 0:TN], in_=XSRC[ic * 128:(ic + 1) * 128, :]), 16)
                    sWF[pb].inc(s.dma_start(out=wf[pb][:, 0:gw * 128],
                                        in_=W[ic * 128:(ic + 1) * 128, grp[0] * 128:(grp[0] + gw) * 128]), 16)
                ldw_n = sWF[pb].n
                ldx_n = sXB[pb].n
                @blk.vector
                def _(v, pb=pb, gw=gw, need=ldw_n, wg=guard["wcast"][pb]):
                    v.wait_ge(sWF[pb].sem, need)
                    v.wait_ge(cMM.sem, wg)
                    cDV.inc(v.tensor_copy(out=wb[pb][:, 0:gw * 128], in_=wf[pb][:, 0:gw * 128]))
                dvn = cDV.n
                @blk.tensor
                def _(t, ic=ic, pb=pb, gw=gw, ntt=ntt, need=dvn, needx=ldx_n, seg=se_gate):
                    t.wait_ge(cDV.sem, need)
                    t.wait_ge(sXB[pb].sem, needx)
                    if ic == 0:
                        t.wait_ge(cSE.sem, seg)
                    for mi in range(gw):
                        for tt in range(ntt):
                            cMM.inc(t.matmul(PP[mi * ntt + tt][:],
                                             wb[pb][:, mi * 128:(mi + 1) * 128],
                                             xb[pb][:, tt * 512:(tt + 1) * 512],
                                             start=(ic == 0), stop=(ic == 31)))
                guard["mm"][pb] = cMM.n
                guard["wcast"][pb] = cMM.n
            if "dq" in stages:
                dq_pump(1)
            mm_done = cMM.n
            for mi, m in enumerate(grp):
                for tt in range(ntt):
                    @blk.scalar
                    def _(sc, pt=PP[mi * ntt + tt], toff=tt * 512, need=mm_done, dn=cDV.n):
                        sc.wait_ge(cMM.sem, need)
                        sc.wait_ge(cDV.sem, dn)
                        cSE.inc(sc.copy(out=qraw[:, toff:toff + 512], in_=pt[:]))
                sen = cSE.n
                @blk.sync
                def _(s, TN=TN, need=sen, dvp=cDV.n):
                    s.wait_ge(cSE.sem, need)
                    s.wait_ge(cDV.sem, dvp)
                    sSW.inc(s.dma_start(out=qswp[0:64, 0:TN], in_=qraw[64:128, 0:TN]), 16)
                    sSW.inc(s.dma_start(out=qswp[64:128, 0:TN], in_=qraw[0:64, 0:TN]), 16)
                ldn2 = sSW.n
                @blk.vector
                def _(v, TN=TN, cost=cost, sint=sint, needL=ldn2, needS=sen):
                    v.wait_ge(sSW.sem, needL)
                    v.wait_ge(cSE.sem, needS)
                    cDV.inc(v.tensor_mul(out=qraw[:, 0:TN], in0=qraw[:, 0:TN], in1=cost[:, 0:TN]))
                    cDV.inc(v.tensor_mul(out=qswp[:, 0:TN], in0=qswp[:, 0:TN], in1=sint[:, 0:TN]))
                    v.wait_ge(cDV.sem, cDV.n)
                    cDV.inc(v.tensor_add(out=qraw[:, 0:TN], in0=qraw[:, 0:TN], in1=qswp[:, 0:TN]))
                    v.wait_ge(cDV.sem, cDV.n)
                    cDV.inc(v.tensor_copy(out=bst[:, 0:TN], in_=qraw[:, 0:TN]))
                dvn3 = cDV.n
                @blk.scalar
                def _(sc, m=m, TN=TN, OUTT=OUTT, need=dvn3):
                    sc.wait_ge(cDV.sem, need)
                    sBST.inc(sc.dma_start(out=OUTT[m * 128:(m + 1) * 128, 0:TN], in_=bst[:, 0:TN]), 16)
                stn = sBST.n
                @blk.vector
                def _(v, need=stn):
                    v.wait_ge(sBST.sem, need)

    def proj_v():
        i0 = 0
        while i0 < 16:
            grp = list(range(i0, min(i0 + 3, 16)))
            i0 += 3
            gw = len(grp)
            se_gate = cSE.n
            for half in range(2):
                for ic in range(32):
                    pb = ic % 2
                    @blk.sync
                    def _(s, ic=ic, pb=pb, half=half, mg=guard["mm"][pb]):
                        s.wait_ge(cMM.sem, mg)
                        sXB[pb].inc(s.dma_start(out=xb[pb][:, 0:TC], in_=XTb[ic * 128:(ic + 1) * 128, :]), 16)
                        sWF[pb].inc(s.dma_start(out=wf[pb][:, 0:512],
                                            in_=WV[ic * 128:(ic + 1) * 128, half * 512:(half + 1) * 512]), 16)
                    ldw_n = sWF[pb].n
                    ldx_n = sXB[pb].n
                    @blk.vector
                    def _(v, pb=pb, need=ldw_n, wg=guard["wcast"][pb]):
                        v.wait_ge(sWF[pb].sem, need)
                        v.wait_ge(cMM.sem, wg)
                        cDV.inc(v.tensor_copy(out=wb[pb][:, 0:512], in_=wf[pb][:, 0:512]))
                    dvn = cDV.n
                    @blk.tensor
                    def _(t, ic=ic, pb=pb, grp=grp, gw=gw, half=half, need=dvn, needx=ldx_n, seg=se_gate):
                        t.wait_ge(cDV.sem, need)
                        t.wait_ge(sXB[pb].sem, needx)
                        if ic == 0:
                            t.wait_ge(cSE.sem, seg)
                        for mi, m in enumerate(grp):
                            cMM.inc(t.matmul(PP[mi * 2 + half][:], xb[pb][:, m * 128:(m + 1) * 128],
                                             wb[pb][:, 0:512], start=(ic == 0), stop=(ic == 31)))
                    guard["mm"][pb] = cMM.n
                    guard["wcast"][pb] = cMM.n
            mm_done = cMM.n
            last_st = [sOBUF.n]
            for mi, m in enumerate(grp):
                for half in range(2):
                    @blk.scalar
                    def _(sc, pt=PP[mi * 2 + half], need=mm_done, lst=last_st[0]):
                        sc.wait_ge(cMM.sem, need)
                        sc.wait_ge(sOBUF.sem, lst)
                        cSE.inc(sc.copy(out=obuf[:], in_=pt[:]))
                    sen = cSE.n
                    @blk.scalar
                    def _(sc, m=m, half=half, need=sen):
                        sc.wait_ge(cSE.sem, need)
                        sOBUF.inc(sc.dma_start(out=VS[m * 128:(m + 1) * 128, half * 512:(half + 1) * 512],
                                             in_=obuf[:]), 16)
                    last_st[0] = sOBUF.n




    FXS = 16
    dq_state = {"steps": [], "i": 0}
    def dq_make_steps():
        steps = dq_state["steps"]
        dqs = dq_state
        dqs.update({"tp_done": [0, 0], "wl_dv": 0, "wh_dv": 0, "wl_st": 0, "wh_st": 0})
        def mslot(j):
            return dq_mux[:, j * HN:(j + 1) * HN]
        def mk_load(ch):
            def load_chunk(ch=ch):
                pb = ch % 2
                @blk.gpsimd
                def _(g, ch=ch, pb=pb):
                    g.wait_ge(cDV.sem, dqs["tp_done"][pb])
                    for p in range(8):
                        sDQ.inc(g.dma_start(out=dq_qw8[pb][:, p * 512:(p + 1) * 512],
                                            in_=QWS[p, :, ch, :].bitcast(U8)), 16)
                    sDQ.inc(g.dma_start(out=dq_l4[pb][:], in_=LUT4S[:, ch, :]), 16)
                    sDQ.inc(g.dma_start(out=dq_l8[pb][:], in_=LUT8S[:, ch, :]), 16)
            return load_chunk
        steps.append(mk_load(0))
        for ch in range(4):
            pb = ch % 2
            def ext_init(ch=ch):
                @blk.vector
                def _(v):
                    v.wait_ge(sDQ.sem, sDQ.n)
                    cDV.inc(v.memset(dq_idx[:], 0))
            steps.append(ext_init)
            for p in range(8):
                def ext_plane(p=p, pb=pb):
                    @blk.vector
                    def _(v, p=p, pb=pb):
                        for r in range(8):
                            m = (1 << r) * 0x01010101
                            if m >= 2 ** 31:
                                m -= 2 ** 32
                            cDV.inc(v.tensor_scalar(out=dq_t8.bitcast(I32)[:, 0:128],
                                                    in0=dq_qw8[pb].bitcast(I32)[:, p * 128:(p + 1) * 128],
                                                    scalar1=m, scalar2=None, op0=Alu.bitwise_and))
                            v.wait_ge(cDV.sem, cDV.n)
                            sl = dq_idx[:].rearrange("o (c r) -> o c r", r=8)[:, :, r]
                            cDV.inc(v.scalar_tensor_tensor(out=sl, in0=dq_t8[:], scalar=float(2.0 ** (p - r)),
                                                           in1=sl, op0=Alu.mult, op1=Alu.add))
                            v.wait_ge(cDV.sem, cDV.n)
                steps.append(ext_plane)
            if ch < 3:
                steps.append(mk_load(ch + 1))
            def tprep(ch=ch, pb=pb):
                @blk.vector
                def _(v, pb=pb):
                    cDV.inc(v.tensor_scalar(out=dq_V16[:], in0=dq_l8[pb][:], scalar1=float(2.0 ** FXS),
                                            scalar2=32768.0, op0=Alu.mult, op1=Alu.add))
                    v.wait_ge(cDV.sem, cDV.n)
                    for sl4 in range(4):
                        base = sl4 * 32
                        cDV.inc(v.tensor_scalar(out=dq_t32[:, 0:256], in0=dq_V16[:], scalar1=4 * sl4,
                                                scalar2=15, op0=Alu.logical_shift_right, op1=Alu.bitwise_and))
                        v.wait_ge(cDV.sem, cDV.n)
                        for k in range(8):
                            sl = dq_t32[:, 0:256].rearrange("o (j k) -> o j k", k=8)[:, :, k]
                            if k == 0:
                                cDV.inc(v.tensor_copy(out=dq_W32[:, base:base + 32], in_=sl))
                            else:
                                cDV.inc(v.tensor_scalar(out=dq_t32[:, 256:288], in0=sl, scalar1=4 * k,
                                                        scalar2=None, op0=Alu.logical_shift_left))
                                v.wait_ge(cDV.sem, cDV.n)
                                cDV.inc(v.tensor_tensor(out=dq_W32[:, base:base + 32],
                                                        in0=dq_W32[:, base:base + 32],
                                                        in1=dq_t32[:, 256:288], op=Alu.bitwise_or))
                            v.wait_ge(cDV.sem, cDV.n)
                    for sl4 in range(4):
                        a = dq_W32[:, sl4 * 32:(sl4 + 1) * 32].rearrange("o (m e) -> o m e", e=2)
                        cDV.inc(v.tensor_tensor(out=dq_XW[:, sl4 * 16:(sl4 + 1) * 16], in0=a[:, :, 0],
                                                in1=a[:, :, 1], op=Alu.bitwise_xor))
                    cDV.inc(v.tensor_copy(out=dq_l4b[:], in_=dq_l4[pb][:]))
                    v.wait_ge(cDV.sem, cDV.n)
                    a4 = dq_l4b.bitcast(I32).rearrange("o (m e) -> o m e", e=2)
                    cDV.inc(v.tensor_tensor(out=dq_XW4[:], in0=a4[:, :, 0], in1=a4[:, :, 1],
                                            op=Alu.bitwise_xor))
                    v.wait_ge(cDV.sem, cDV.n)
                dqs["tp_done"][pb] = cDV.n
            steps.append(tprep)
            for q in range(4):
                def quarter_pre(ch=ch, q=q):
                    @blk.vector
                    def _(v, q=q):
                        v.wait_ge(sDQS.sem, dqs["wh_st"])
                        cDV.inc(v.tensor_copy(out=dq_ci[:], in_=dq_idx[:, q * HN:(q + 1) * HN]))
                        v.wait_ge(cDV.sem, cDV.n)
                        cDV.inc(v.tensor_scalar(out=dq_msk[:, 0:HN], in0=dq_ci[:], scalar1=8, scalar2=None,
                                                op0=Alu.bitwise_and))
                        v.wait_ge(cDV.sem, cDV.n)
                        cDV.inc(v.tensor_scalar(out=dq_msk[:, 0:HN], in0=dq_msk[:, 0:HN], scalar1=-0.125,
                                                scalar2=None, op0=Alu.mult))
                        for k in range(1, 5):
                            cDV.inc(v.tensor_scalar(out=dq_msk[:, k * HN:(k + 1) * HN], in0=dq_ci[:],
                                                    scalar1=(1 << (k + 3)), scalar2=None, op0=Alu.bitwise_and))
                        cDV.inc(v.tensor_scalar(out=dq_sham[:], in0=dq_ci[:], scalar1=7, scalar2=None,
                                                op0=Alu.bitwise_and))
                        v.wait_ge(cDV.sem, cDV.n)
                        cDV.inc(v.tensor_scalar(out=dq_sham[:], in0=dq_sham[:], scalar1=4.0, scalar2=None,
                                                op0=Alu.mult))
                        v.wait_ge(cDV.sem, cDV.n)
                steps.append(quarter_pre)
                def wl_step(ch=ch, q=q):
                    @blk.vector
                    def _(v):
                        v.wait_ge(sDQS.sem, dqs["wl_st"])
                        cDV.inc(v.tensor_scalar(out=mslot(0), in0=dq_ci[:], scalar1=2, scalar2=None,
                                                op0=Alu.bitwise_and))
                        v.wait_ge(cDV.sem, cDV.n)
                        cDV.inc(v.tensor_scalar(out=mslot(0), in0=mslot(0), scalar1=-0.5, scalar2=None,
                                                op0=Alu.mult))
                        cDV.inc(v.tensor_scalar(out=mslot(1), in0=dq_ci[:], scalar1=4, scalar2=None,
                                                op0=Alu.bitwise_and))
                        v.wait_ge(cDV.sem, cDV.n)
                        lvl = {}
                        free = [2, 3, 4]
                        for mI in range(4):
                            s_new = free.pop()
                            cDV.inc(v.tensor_scalar(out=mslot(s_new), in0=mslot(0),
                                                    scalar1=dq_XW4[:, mI:mI + 1],
                                                    scalar2=dq_l4b.bitcast(I32)[:, 2 * mI:2 * mI + 1],
                                                    op0=Alu.bitwise_and, op1=Alu.bitwise_xor))
                            v.wait_ge(cDV.sem, cDV.n)
                            lev = 1
                            while lev in lvl:
                                s_old = lvl.pop(lev)
                                mk = mslot(1) if lev == 1 else dq_msk[:, 0:HN]
                                cDV.inc(v.copy_predicated(out=mslot(s_old), mask=mk, data=mslot(s_new)))
                                v.wait_ge(cDV.sem, cDV.n)
                                free.append(s_new)
                                s_new = s_old
                                lev += 1
                            lvl[lev] = s_new
                        res = mslot(lvl[3])
                        cDV.inc(v.tensor_scalar(out=mslot(0), in0=dq_ci[:], scalar1=1, scalar2=None,
                                                op0=Alu.bitwise_and))
                        v.wait_ge(cDV.sem, cDV.n)
                        cDV.inc(v.tensor_scalar(out=mslot(0), in0=mslot(0), scalar1=16.0, scalar2=None,
                                                op0=Alu.mult))
                        v.wait_ge(cDV.sem, cDV.n)
                        cDV.inc(v.tensor_tensor(out=res, in0=res, in1=mslot(0), op=Alu.logical_shift_right))
                        v.wait_ge(cDV.sem, cDV.n)
                        cDV.inc(v.tensor_scalar(out=res, in0=res, scalar1=0xFFFF, scalar2=None,
                                                op0=Alu.bitwise_and))
                        v.wait_ge(cDV.sem, cDV.n)
                        rsl = res.bitcast(U16).rearrange("o (i two) -> o i two", two=2)[:, :, 0]
                        cDV.inc(v.tensor_copy(out=dq_wl.bitcast(U16)[:, 0:HN], in_=rsl))
                        v.wait_ge(cDV.sem, cDV.n)
                    dqs["wl_dv"] = cDV.n
                steps.append(wl_step)
                for sl4 in range(4):
                    def mux_slice(ch=ch, q=q, sl4=sl4):
                        @blk.vector
                        def _(v, sl4=sl4):
                            lvl = {}
                            free = [0, 1, 2, 3, 4]
                            for mm_ in range(16):
                                s_new = free.pop()
                                cDV.inc(v.tensor_scalar(out=mslot(s_new), in0=dq_msk[:, 0:HN],
                                                        scalar1=dq_XW[:, sl4 * 16 + mm_:sl4 * 16 + mm_ + 1],
                                                        scalar2=dq_W32[:, sl4 * 32 + 2 * mm_:sl4 * 32 + 2 * mm_ + 1],
                                                        op0=Alu.bitwise_and, op1=Alu.bitwise_xor))
                                v.wait_ge(cDV.sem, cDV.n)
                                lev = 1
                                while lev in lvl:
                                    s_old = lvl.pop(lev)
                                    cDV.inc(v.copy_predicated(out=mslot(s_old),
                                                              mask=dq_msk[:, lev * HN:(lev + 1) * HN],
                                                              data=mslot(s_new)))
                                    v.wait_ge(cDV.sem, cDV.n)
                                    free.append(s_new)
                                    s_new = s_old
                                    lev += 1
                                lvl[lev] = s_new
                            res = mslot(lvl[5])
                            if sl4 == 0:
                                cDV.inc(v.tensor_tensor(out=dq_acc[:], in0=res, in1=dq_sham[:],
                                                        op=Alu.logical_shift_right))
                                v.wait_ge(cDV.sem, cDV.n)
                                cDV.inc(v.tensor_scalar(out=dq_acc[:], in0=dq_acc[:], scalar1=15,
                                                        scalar2=None, op0=Alu.bitwise_and))
                            else:
                                cDV.inc(v.tensor_tensor(out=dq_t32[:], in0=res, in1=dq_sham[:],
                                                        op=Alu.logical_shift_right))
                                v.wait_ge(cDV.sem, cDV.n)
                                cDV.inc(v.tensor_scalar(out=dq_t32[:], in0=dq_t32[:], scalar1=15,
                                                        scalar2=None, op0=Alu.bitwise_and))
                                v.wait_ge(cDV.sem, cDV.n)
                                cDV.inc(v.tensor_scalar(out=dq_t32[:], in0=dq_t32[:],
                                                        scalar1=float(2.0 ** (4 * sl4)),
                                                        scalar2=None, op0=Alu.mult))
                                v.wait_ge(cDV.sem, cDV.n)
                                cDV.inc(v.tensor_tensor(out=dq_acc[:], in0=dq_acc[:], in1=dq_t32[:], op=Alu.add))
                            v.wait_ge(cDV.sem, cDV.n)
                    steps.append(mux_slice)
                    if sl4 == 1:
                        def wl_store(ch=ch, q=q):
                            @blk.gpsimd
                            def _(g, ch=ch, q=q):
                                g.wait_ge(cDV.sem, dqs["wl_dv"])
                                sDQS.inc(g.dma_start(out=AGI[ch * 128:(ch + 1) * 128, q * HN:(q + 1) * HN],
                                                     in_=dq_wl[:]), 16)
                            dqs["wl_st"] = sDQS.n
                        steps.append(wl_store)
                def wh_fin(ch=ch, q=q):
                    @blk.vector
                    def _(v):
                        v.wait_ge(sDQS.sem, dqs["wh_st"])
                        cDV.inc(v.tensor_scalar(out=dq_wh[:], in0=dq_acc[:], scalar1=32768.0,
                                                scalar2=float(2.0 ** -FXS), op0=Alu.subtract, op1=Alu.mult))
                        v.wait_ge(cDV.sem, cDV.n)
                    dqs["wh_dv"] = cDV.n
                steps.append(wh_fin)
                def wh_store(ch=ch, q=q):
                    @blk.gpsimd
                    def _(g, ch=ch, q=q):
                        g.wait_ge(cDV.sem, dqs["wh_dv"])
                        sDQS.inc(g.dma_start(out=AGI[ch * 128:(ch + 1) * 128, 4096 + q * HN:4096 + (q + 1) * HN],
                                             in_=dq_wh[:]), 16)
                    dqs["wh_st"] = sDQS.n
                steps.append(wh_store)
    def dq_pump(k):
        st = dq_state["steps"]
        while k > 0 and dq_state["i"] < len(st):
            st[dq_state["i"]]()
            dq_state["i"] += 1
            k -= 1
    def dq_finish():
        dq_pump(len(dq_state["steps"]))
        if fake_ag:
            @blk.gpsimd
            def _(g, need=sDQS.n):
                g.wait_ge(sDQS.sem, need)
                for c8 in range(8):
                    sDQS.inc(g.dma_start(out=AGO[c8 * 512:(c8 + 1) * 512, :], in_=AGI[:]), 16)
                cAG.inc(g.sem_inc(cAG.sem, 0)) if False else None
            @blk.gpsimd
            def _(g, need=sDQS.n):
                g.wait_ge(sDQS.sem, need)
                cAG.inc(g.engine_nop()) if False else None
                cAG.n += 1
                g.sem_inc(cAG.sem, 1)
        else:
            @blk.gpsimd
            def _(g, need=sDQS.n):
                g.wait_ge(sDQS.sem, need)
                cAG.inc(g.collective_compute(
                    "AllGather", mybir.AluOpType.bypass,
                    replica_groups=[list(range(8))],
                    ins=[AGI[:]],
                    outs=[AGO[:]],
                ))

    def p2_attention():
        sqs_b = bst
        von = sOBUF.n
        bstn = sBST.n
        for kh in range(8):
            if "dq" in stages:
                dq_pump(2)
            @blk.sync
            def _(s, kh=kh, dv=cDV.n, se=cSE.n, mm=cMM.n, von=von, bstn=bstn):
                s.wait_ge(cDV.sem, dv)
                s.wait_ge(cMM.sem, mm)
                s.wait_ge(sOBUF.sem, von)
                s.wait_ge(sBST.sem, bstn)
                sKH.inc(s.dma_start(out=kh_t[:], in_=KT[kh * 128:(kh + 1) * 128, :]), 16)
                for tkc in range(16):
                    sVH.inc(s.dma_start(out=vh[:, tkc * 128:(tkc + 1) * 128],
                                        in_=VS[tkc * 128:(tkc + 1) * 128, kh * 128:(kh + 1) * 128]), 16)
            khn = sKH.n
            vhn = sVH.n
            @blk.scalar
            def _(sc, need=khn):
                sc.wait_ge(sKH.sem, need)
                cSE.inc(sc.activation(out=sqs_b[:, 0:TC], in_=kh_t[:], func=Act.Square))
            sen = cSE.n
            for q4 in range(4):
                @blk.tensor
                def _(t, q4=q4, need=cSE.n):
                    t.wait_ge(cSE.sem, need)
                    cMM.inc(t.matmul(PS0[:], onesb[:], sqs_b[:, q4 * 512:(q4 + 1) * 512],
                                     start=True, stop=True))
                mmn = cMM.n
                @blk.scalar
                def _(sc, q4=q4, need=mmn):
                    sc.wait_ge(cMM.sem, need)
                    cSE.inc(sc.copy(out=rowa[:, q4 * 512:(q4 + 1) * 512], in_=PS0[:]))
            sen4 = cSE.n
            @blk.vector
            def _(v, need=sen4):
                v.wait_ge(cSE.sem, need)
                cDV.inc(v.reduce_max(out=rowb[:, 0:1], in_=rowa[:, 0:TC], axis=mybir.AxisListType.X))
            mkn = cDV.n
            kmm_gate = [cMM.n]

            for hq in range(4):
                h = kh * 4 + hq
                @blk.sync
                def _(s, h=h, dv=cDV.n, mm=cMM.n):
                    s.wait_ge(cDV.sem, dv)
                    s.wait_ge(cMM.sem, mm)
                    sQH.inc(s.dma_start(out=qh[:], in_=QT[h * 128:(h + 1) * 128, :]), 16)
                qhn = sQH.n
                @blk.scalar
                def _(sc, need=qhn, mmg=kmm_gate[0]):
                    sc.wait_ge(sQH.sem, need)
                    sc.wait_ge(cMM.sem, mmg)
                    cSE.inc(sc.activation(out=sqs_b[:, 0:TO], in_=qh[:], func=Act.Square))
                sen2 = cSE.n
                for q2 in range(2):
                    @blk.tensor
                    def _(t, q2=q2, need=cSE.n):
                        t.wait_ge(cSE.sem, need)
                        cMM.inc(t.matmul(PS0[:], onesb[:], sqs_b[:, q2 * 512:(q2 + 1) * 512],
                                         start=True, stop=True))
                    mmn2 = cMM.n
                    @blk.scalar
                    def _(sc, q2=q2, need=mmn2):
                        sc.wait_ge(cMM.sem, need)
                        cSE.inc(sc.copy(out=rowa[:, 1024 + q2 * 512:1024 + (q2 + 1) * 512], in_=PS0[:]))
                kmm_gate[0] = cMM.n
                seq0 = cSE.n
                @blk.vector
                def _(v, need=seq0, needm=mkn):
                    v.wait_ge(cSE.sem, need)
                    v.wait_ge(cDV.sem, needm)
                    cDV.inc(v.tensor_scalar(out=rowa[:, 1024:1024 + TO], in0=rowa[:, 1024:1024 + TO],
                                            scalar1=rowb[:, 0:1], scalar2=None, op0=Alu.mult))
                dvq2 = cDV.n
                @blk.scalar
                def _(sc, need=dvq2):
                    sc.wait_ge(cDV.sem, need)
                    cSE.inc(sc.activation(out=negMf[:], in_=rowa[:, 1024:1024 + TO], func=Act.Sqrt))
                seq = cSE.n
                @blk.vector
                def _(v, need=seq):
                    v.wait_ge(cSE.sem, need)
                    cDV.inc(v.tensor_scalar(out=negMb[:], in0=negMf[:], scalar1=-1.01, scalar2=None,
                                            op0=Alu.mult))
                negn = cDV.n
                if "dq" in stages:
                    dq_pump(2)

                for tt in range(2):
                    ntk = 8 * tt + 8
                    zm = cMM.n
                    @blk.scalar
                    def _(sc, mmp=zm):
                        sc.wait_ge(cMM.sem, mmp)
                        cSE.inc(sc.memzero(at_[0][:, 0:384]))
                        cSE.inc(sc.memzero(at_[1][:, 0:384]))
                    expg = [cSE.n, cSE.n]
                    dv0 = cDV.n
                    for tkc in range(ntk - 1, -1, -1):
                        jj = max(0, (tkc >> 1) - 4 * tt)
                        noff = jj * 128
                        sp = PP[tkc % 2]
                        atb = at_[tkc % 2]
                        j = tkc >> 1
                        diag = (4 * tt <= j < 4 * tt + 4)
                        @blk.tensor
                        def _(t, sp=sp, tkc=tkc, tt=tt, noff=noff, neg=negn, qn=qhn, kn=khn,
                              se=expg[tkc % 2], diag=diag, sel=tkc & 1,
                              first=(tkc == ntk - 1), dv0=dv0):
                            t.wait_ge(cDV.sem, neg)
                            if first:
                                t.wait_ge(cDV.sem, dv0)
                            t.wait_ge(sQH.sem, qn)
                            t.wait_ge(sKH.sem, kn)
                            t.wait_ge(cSE.sem, se)
                            cMM.inc(t.matmul(sp[:, noff:512], onesr1[:],
                                             negMb[0:1, tt * 512 + noff:(tt + 1) * 512],
                                             start=True, stop=False))
                            if diag:
                                cMM.inc(t.matmul(sp[:, noff:noff + 128], identb[:],
                                                 masktb[:, sel * 128:(sel + 1) * 128],
                                                 start=False, stop=False))
                            cMM.inc(t.matmul(sp[:, noff:512], kh_t[:, tkc * 128:(tkc + 1) * 128],
                                             qh[:, tt * 512 + noff:(tt + 1) * 512],
                                             start=False, stop=True))
                        mmn3 = cMM.n
                        @blk.scalar
                        def _(sc, sp=sp, atb=atb, noff=noff, need=mmn3):
                            sc.wait_ge(cMM.sem, need)
                            cSE.inc(sc.activation(out=atb[:, noff:512], in_=sp[:, noff:512],
                                                  func=Act.Exp, scale=SC))
                        expg[tkc % 2] = cSE.n
                        expn = cSE.n
                        @blk.tensor
                        def _(t, atb=atb, tkc=tkc, ntk=ntk, need=expn, vn=vhn):
                            t.wait_ge(cSE.sem, need)
                            t.wait_ge(sVH.sem, vn)
                            cMM.inc(t.matmul(PP[2][:], vh[:, tkc * 128:(tkc + 1) * 128], atb[:],
                                             start=(tkc == ntk - 1), stop=(tkc == 0)))
                            cMM.inc(t.matmul(PS1[:], onesb[:], atb[:],
                                             start=(tkc == ntk - 1), stop=(tkc == 0)))
                        if "dq" in stages and tkc % 4 == 2:
                            dq_pump(1)
                    avn = cMM.n
                    @blk.scalar
                    def _(sc, need=avn, dv=cDV.n):
                        sc.wait_ge(cMM.sem, need)
                        sc.wait_ge(cDV.sem, dv)
                        cSE.inc(sc.copy(out=rrow[:], in_=PS1[:]))
                    sen3 = cSE.n
                    @blk.vector
                    def _(v, need=sen3):
                        v.wait_ge(cSE.sem, need)
                        cDV.inc(v.reciprocal(out=rrow[:], in_=rrow[:]))
                    rcn = cDV.n
                    @blk.tensor
                    def _(t, need=rcn):
                        t.wait_ge(cDV.sem, need)
                        cMM.inc(t.matmul(PP[4][:], onesr1F[:], rrow[:], start=True, stop=True))
                    rbn = cMM.n
                    @blk.scalar
                    def _(sc, need=rbn, dv=cDV.n):
                        sc.wait_ge(cMM.sem, need)
                        sc.wait_ge(cDV.sem, dv)
                        cSE.inc(sc.copy(out=rtile[:], in_=PP[4][:]))
                    rbe = cSE.n
                    @blk.vector
                    def _(v, need=rbe, ob=sOB.n):
                        v.wait_ge(cSE.sem, need)
                        v.wait_ge(sOB.sem, ob)
                        cDV.inc(v.tensor_mul(out=obuf[:], in0=PP[2][:], in1=rtile[:]))
                    obn = cDV.n
                    @blk.scalar
                    def _(sc, h=h, tt=tt, need=obn):
                        sc.wait_ge(cDV.sem, need)
                        sOB.inc(sc.dma_start(out=OBT[h * 128:(h + 1) * 128, tt * 512:(tt + 1) * 512],
                                             in_=obuf[:]), 16)
                    if "dq" in stages:
                        dq_pump(1)

    def p2b_wo():
        obn_all = sOB.n
        @blk.sync
        def _(s, need=obn_all):
            s.wait_ge(sOB.sem, need)
        @blk.vector
        def _(v):
            cDV.inc(v.memset(s1row[:], 0.0))
            cDV.inc(v.memset(s2row[:], 0.0))
        i0 = 0
        while i0 < 32:
            grp = list(range(i0, min(i0 + 3, 32)))
            i0 += 3
            gw = len(grp)
            se_gate = cSE.n
            dv_gate = cDV.n
            for dh in range(32):
                pb = dh % 2
                @blk.sync
                def _(s, dh=dh, pb=pb, grp=grp, gw=gw, mg=guard["mm"][pb]):
                    s.wait_ge(cMM.sem, mg)
                    sOL[pb].inc(s.dma_start(out=x1r[pb][:], in_=OBT[dh * 128:(dh + 1) * 128, :]), 16)
                    sWF[pb].inc(s.dma_start(out=wf[pb][:, 0:gw * 128],
                                            in_=WO[dh * 128:(dh + 1) * 128, grp[0] * 128:(grp[0] + gw) * 128]), 16)
                ldw_n = sWF[pb].n
                ldx_n = sOL[pb].n
                @blk.vector
                def _(v, pb=pb, gw=gw, need=ldw_n, wg=guard["wcast"][pb]):
                    v.wait_ge(sWF[pb].sem, need)
                    v.wait_ge(cMM.sem, wg)
                    cDV.inc(v.tensor_copy(out=wb[pb][:, 0:gw * 128], in_=wf[pb][:, 0:gw * 128]))
                dvn = cDV.n
                @blk.tensor
                def _(t, dh=dh, pb=pb, gw=gw, need=dvn, needx=ldx_n, seg=se_gate, dvg=dv_gate):
                    t.wait_ge(cDV.sem, need)
                    t.wait_ge(sOL[pb].sem, needx)
                    if dh == 0:
                        t.wait_ge(cSE.sem, seg)
                        t.wait_ge(cDV.sem, dvg)
                    for mi in range(gw):
                        for tt2 in range(2):
                            cMM.inc(t.matmul(PP[mi * 2 + tt2][:], wb[pb][:, mi * 128:(mi + 1) * 128],
                                             x1r[pb][:, tt2 * 512:(tt2 + 1) * 512],
                                             start=(dh == 0), stop=(dh == 31)))
                guard["mm"][pb] = cMM.n
                guard["wcast"][pb] = cMM.n
            mm_done = cMM.n
            for mi, m in enumerate(grp):
                for tt2 in range(2):
                    ptile = PP[mi * 2 + tt2]
                    @blk.sync
                    def _(s, m=m, tt2=tt2, dv=cDV.n):
                        s.wait_ge(cDV.sem, dv)
                        sRT.inc(s.dma_start(out=rtile2[:], in_=XTO[m * 128:(m + 1) * 128,
                                                                  tt2 * 512:(tt2 + 1) * 512]), 16)
                    rtn = sRT.n
                    @blk.vector
                    def _(v, ptile=ptile, need=mm_done, nrt=rtn, dvp=cDV.n, stp=sX1.n):
                        v.wait_ge(cMM.sem, need)
                        v.wait_ge(sRT.sem, nrt)
                        v.wait_ge(sX1.sem, stp)
                        cDV.inc(v.tensor_add(out=rtile[:], in0=ptile[:], in1=rtile2[:]))
                    dvr = cDV.n
                    @blk.scalar
                    def _(sc, m=m, tt2=tt2, need=dvr):
                        sc.wait_ge(cDV.sem, need)
                        sX1.inc(sc.dma_start(out=R1[m * 128:(m + 1) * 128, tt2 * 512:(tt2 + 1) * 512],
                                             in_=rtile[:]), 16)
                        cSE.inc(sc.activation(out=rtile2[:], in_=rtile[:], func=Act.Square))
                    sqn = cSE.n
                    @blk.tensor
                    def _(t, need=dvr, nsq=sqn, dvp=cDV.n):
                        t.wait_ge(cDV.sem, dvp)
                        t.wait_ge(cDV.sem, need)
                        cMM.inc(t.matmul(PS0[:], onesbF[:], rtile[:], start=True, stop=True))
                        t.wait_ge(cSE.sem, nsq)
                        cMM.inc(t.matmul(PS1[:], onesbF[:], rtile2[:], start=True, stop=True))
                    smn = cMM.n
                    @blk.vector
                    def _(v, tt2=tt2, need=smn):
                        v.wait_ge(cMM.sem, need)
                        cDV.inc(v.tensor_add(out=s1row[0:1, tt2 * 512:(tt2 + 1) * 512],
                                             in0=s1row[0:1, tt2 * 512:(tt2 + 1) * 512], in1=PS0[:]))
                        cDV.inc(v.tensor_add(out=s2row[0:1, tt2 * 512:(tt2 + 1) * 512],
                                             in0=s2row[0:1, tt2 * 512:(tt2 + 1) * 512], in1=PS1[:]))

        @blk.vector
        def _(v):
            v.wait_ge(cDV.sem, cDV.n)
            cDV.inc(v.tensor_scalar(out=mrow[:], in0=s1row[:], scalar1=float(1.0 / 4096.0), scalar2=None,
                                    op0=Alu.mult))
            v.wait_ge(cDV.sem, cDV.n)
            cDV.inc(v.tensor_mul(out=rowa[:, 0:TO], in0=mrow[:], in1=mrow[:]))
            cDV.inc(v.tensor_scalar(out=rowb[:, 0:TO], in0=s2row[:], scalar1=float(1.0 / 4096.0),
                                    scalar2=None, op0=Alu.mult))
            v.wait_ge(cDV.sem, cDV.n)
            cDV.inc(v.tensor_sub(out=rowb[:, 0:TO], in0=rowb[:, 0:TO], in1=rowa[:, 0:TO]))
        vvn = cDV.n
        @blk.vector
        def _(v, need=vvn):
            v.wait_ge(cDV.sem, need)
            cDV.inc(v.tensor_scalar(out=rowb[0:1, 0:TO], in0=rowb[0:1, 0:TO], scalar1=float(EPS),
                                    scalar2=None, op0=Alu.add))
        vvn2 = cDV.n
        @blk.scalar
        def _(sc, need=vvn2):
            sc.wait_ge(cDV.sem, need)
            cSE.inc(sc.activation(out=rowb[0:1, 0:TO], in_=rowb[0:1, 0:TO], func=Act.Sqrt))
        ssn = cSE.n
        @blk.vector
        def _(v, need=ssn):
            v.wait_ge(cSE.sem, need)
            cDV.inc(v.reciprocal(out=rstd[:], in_=rowb[0:1, 0:TO]))
            v.wait_ge(cDV.sem, cDV.n)
            cDV.inc(v.tensor_scalar(out=rowa[:, 0:TO], in0=rowa[:, 0:TO], scalar1=4096.0, scalar2=None,
                                    op0=Alu.mult))
            v.wait_ge(cDV.sem, cDV.n)
            cDV.inc(v.tensor_sub(out=rowa[:, 0:TO], in0=s2row[:], in1=rowa[:, 0:TO]))
            cDV.inc(v.tensor_mul(out=rowb[0:1, 0:TO], in0=rstd[:], in1=rstd[:]))
            v.wait_ge(cDV.sem, cDV.n)
            cDV.inc(v.tensor_mul(out=rowa[:, 0:TO], in0=rowa[:, 0:TO], in1=rowb[0:1, 0:TO]))
        een = cDV.n
        @blk.scalar
        def _(sc, need=een):
            sc.wait_ge(cDV.sem, need)
            cSE.inc(sc.activation(out=erow[:], in_=rowa[0:1, 0:TO], func=Act.Sqrt))
        esn = cSE.n
        @blk.vector
        def _(v, need=esn):
            v.wait_ge(cSE.sem, need)
            cDV.inc(v.tensor_scalar(out=mskrow[:], in0=erow[:], scalar1=64.0, scalar2=None, op0=Alu.is_gt))
        mbn = cDV.n
        for q2 in range(2):
            @blk.tensor
            def _(t, q2=q2, need=mbn, dv=cDV.n):
                t.wait_ge(cDV.sem, max(need, dv))
                cMM.inc(t.matmul(PP[4][:], onesr1F[:], mskrow[0:1, q2 * 512:(q2 + 1) * 512],
                                 start=True, stop=True))
            mqn = cMM.n
            @blk.vector
            def _(v, q2=q2, need=mqn):
                v.wait_ge(cMM.sem, need)
                cDV.inc(v.tensor_copy(out=maskbc[:, q2 * 512:(q2 + 1) * 512], in_=PP[4][:]))

    def p2c_ln1():
        for tt2 in range(2):
            @blk.tensor
            def _(t, tt2=tt2, need=cDV.n):
                t.wait_ge(cDV.sem, need)
                cMM.inc(t.matmul(PP[4][:], onesr1F[:], mrow[0:1, tt2 * 512:(tt2 + 1) * 512],
                                 start=True, stop=True))
                cMM.inc(t.matmul(PP[5][:], onesr1F[:], rstd[0:1, tt2 * 512:(tt2 + 1) * 512],
                                 start=True, stop=True))
            bcn = cMM.n
            for do in range(32):
                @blk.sync
                def _(s, do=do, tt2=tt2, dv=cDV.n, stp=sX1.n):
                    s.wait_ge(cDV.sem, dv)
                    s.wait_ge(sX1.sem, stp)
                    sRT.inc(s.dma_start(out=rtile[:], in_=R1[do * 128:(do + 1) * 128,
                                                           tt2 * 512:(tt2 + 1) * 512]), 16)
                rtn = sRT.n
                @blk.vector
                def _(v, need=rtn, bc=bcn, stp=sX1.n):
                    v.wait_ge(sRT.sem, need)
                    v.wait_ge(cMM.sem, bc)
                    v.wait_ge(sX1.sem, stp)
                    cDV.inc(v.tensor_sub(out=rtile[:], in0=rtile[:], in1=PP[4][:]))
                    v.wait_ge(cDV.sem, cDV.n)
                    cDV.inc(v.tensor_mul(out=rtile[:], in0=rtile[:], in1=PP[5][:]))
                    v.wait_ge(cDV.sem, cDV.n)
                    cDV.inc(v.tensor_copy(out=obuf[:], in_=rtile[:]))
                dvn = cDV.n
                @blk.scalar
                def _(sc, do=do, tt2=tt2, need=dvn):
                    sc.wait_ge(cDV.sem, need)
                    sX1.inc(sc.dma_start(out=X1F[do * 128:(do + 1) * 128, tt2 * 512:(tt2 + 1) * 512],
                                         in_=rtile[:]), 16)
                    sX1.inc(sc.dma_start(out=X1B[do * 128:(do + 1) * 128, tt2 * 512:(tt2 + 1) * 512],
                                         in_=obuf[:]), 16)

    if "dq" in stages:
        dq_make_steps()
    if "p1" in stages:
        @blk.sync
        def _(s):
            s.wait_ge(cGD.sem, n_cast)
        proj_std(32, TO, WQ, XTOb, QT, cosO, sinO, 3)
        proj_std(8, TC, WK, XTb, KT, cosC, sinC, 1)
        proj_v()
    kh_t = kh
    if "p2" in stages:
        p2_attention()
    if "dq" in stages:
        dq_finish()
    es_dq.close()

    def p3_ffn():
        x1res = es.enter_context(nc.sbuf_tensor("x1res", [128, 32 * 1024], BF16))
        ylt = [es.enter_context(nc.sbuf_tensor(f"ylt{i}", [128, 512], F32)) for i in range(6)]
        wtt = [es.enter_context(nc.sbuf_tensor(f"wtt{i}", [128, 384], BF16)) for i in range(2)]
        @blk.sync
        def _(s, agn=cAG.n, x1n=sX1.n):
            s.wait_ge(cAG.sem, agn)
            s.wait_ge(sX1.sem, x1n)
            for ic in range(32):
                sXR.inc(s.dma_start(out=x1res[:, ic * 1024:(ic + 1) * 1024],
                                    in_=X1B[ic * 128:(ic + 1) * 128, :]), 16)
        xrn = sXR.n
        @blk.vector
        def _(v, dv=cDV.n):
            v.wait_ge(cDV.sem, dv)
            cDV.inc(v.memset(s1row[:], 0.0))
            cDV.inc(v.memset(s2row[:], 0.0))
        i0 = 0
        while i0 < 32:
            grp = list(range(i0, min(i0 + 3, 32)))
            i0 += 3
            gw = len(grp)
            ylts = {}
            for tbl in range(2):
                se_gate = cSE.n
                dv_gate = cDV.n
                for ic in range(32):
                    pb = ic % 2
                    @blk.sync
                    def _(s, ic=ic, pb=pb, grp=grp, gw=gw, tbl=tbl, mg=guard["mm"][pb], xn=xrn):
                        s.wait_ge(cMM.sem, mg)
                        s.wait_ge(sXR.sem, xn)
                        sWF[pb].inc(s.dma_start(
                            out=wtt[pb][:, 0:gw * 128],
                            in_=AGO[grp[0] * 128:(grp[0] + gw) * 128,
                                    tbl * 4096 + ic * 128:tbl * 4096 + (ic + 1) * 128],
                            transpose=True), 16)
                    ldw_n = sWF[pb].n
                    @blk.tensor
                    def _(t, ic=ic, pb=pb, gw=gw, need=ldw_n, seg=se_gate, dvg=dv_gate, tbl=tbl):
                        t.wait_ge(sWF[pb].sem, need)
                        if ic == 0:
                            t.wait_ge(cSE.sem, seg)
                            t.wait_ge(cDV.sem, dvg)
                        for mi in range(gw):
                            for tt2 in range(2):
                                cMM.inc(t.matmul(PP[mi * 2 + tt2][:], wtt[pb][:, mi * 128:(mi + 1) * 128],
                                                 x1res[:, ic * 1024 + tt2 * 512:ic * 1024 + (tt2 + 1) * 512],
                                                 start=(ic == 0), stop=(ic == 31)))
                    guard["mm"][pb] = cMM.n
                mm_done = cMM.n
                if tbl == 0:
                    for mi in range(gw):
                        for tt2 in range(2):
                            @blk.scalar
                            def _(sc, k=mi * 2 + tt2, need=mm_done, dv=cDV.n):
                                sc.wait_ge(cMM.sem, need)
                                sc.wait_ge(cDV.sem, dv)
                                cSE.inc(sc.copy(out=ylt[k][:], in_=PP[k][:]))
                else:
                    for mi, m in enumerate(grp):
                        for tt2 in range(2):
                            k = mi * 2 + tt2
                            @blk.sync
                            def _(s, m=m, tt2=tt2, dv=cDV.n, stp=sX1.n):
                                s.wait_ge(cDV.sem, dv)
                                s.wait_ge(sX1.sem, stp)
                                sRT.inc(s.dma_start(out=rtile2[:], in_=X1F[m * 128:(m + 1) * 128,
                                                    tt2 * 512:(tt2 + 1) * 512]), 16)
                            rtn = sRT.n
                            @blk.vector
                            def _(v, k=k, tt2=tt2, need=mm_done, se=cSE.n, stp=sX1.n):
                                v.wait_ge(cMM.sem, need)
                                v.wait_ge(cSE.sem, se)
                                v.wait_ge(sX1.sem, stp)
                                cDV.inc(v.tensor_sub(out=rtile[:], in0=PP[k][:], in1=ylt[k][:]))
                                v.wait_ge(cDV.sem, cDV.n)
                                cDV.inc(v.tensor_mul(out=rtile[:], in0=rtile[:],
                                                     in1=maskbc[:, tt2 * 512:(tt2 + 1) * 512]))
                                v.wait_ge(cDV.sem, cDV.n)
                                cDV.inc(v.tensor_add(out=rtile[:], in0=rtile[:], in1=ylt[k][:]))
                            dvy = cDV.n
                            @blk.vector
                            def _(v, m=m, need=dvy, nrt=rtn):
                                v.wait_ge(cDV.sem, need)
                                v.wait_ge(sRT.sem, nrt)
                                cDV.inc(v.tensor_scalar(out=rtile[:], in0=rtile[:], scalar1=biask[:, m:m + 1],
                                                        scalar2=None, op0=Alu.add))
                                v.wait_ge(cDV.sem, cDV.n)
                                cDV.inc(v.tensor_add(out=rtile[:], in0=rtile[:], in1=rtile2[:]))
                            dvr = cDV.n
                            @blk.scalar
                            def _(sc, m=m, tt2=tt2, need=dvr):
                                sc.wait_ge(cDV.sem, need)
                                sX1.inc(sc.dma_start(out=R2[m * 128:(m + 1) * 128,
                                                            tt2 * 512:(tt2 + 1) * 512], in_=rtile[:]), 16)
                                cSE.inc(sc.activation(out=rtile2[:], in_=rtile[:], func=Act.Square))
                            sqn = cSE.n
                            @blk.tensor
                            def _(t, need=dvr, nsq=sqn, dvp=cDV.n):
                                t.wait_ge(cDV.sem, dvp)
                                cMM.inc(t.matmul(PS0[:], onesbF[:], rtile[:], start=True, stop=True))
                                t.wait_ge(cSE.sem, nsq)
                                cMM.inc(t.matmul(PS1[:], onesbF[:], rtile2[:], start=True, stop=True))
                            smn = cMM.n
                            @blk.vector
                            def _(v, tt2=tt2, need=smn):
                                v.wait_ge(cMM.sem, need)
                                cDV.inc(v.tensor_add(out=s1row[0:1, tt2 * 512:(tt2 + 1) * 512],
                                                     in0=s1row[0:1, tt2 * 512:(tt2 + 1) * 512], in1=PS0[:]))
                                cDV.inc(v.tensor_add(out=s2row[0:1, tt2 * 512:(tt2 + 1) * 512],
                                                     in0=s2row[0:1, tt2 * 512:(tt2 + 1) * 512], in1=PS1[:]))
        @blk.vector
        def _(v):
            v.wait_ge(cDV.sem, cDV.n)
            cDV.inc(v.tensor_scalar(out=m2row[:], in0=s1row[:], scalar1=float(1.0 / 4096.0), scalar2=None,
                                    op0=Alu.mult))
            v.wait_ge(cDV.sem, cDV.n)
            cDV.inc(v.tensor_mul(out=rowa[:, 0:TO], in0=m2row[:], in1=m2row[:]))
            cDV.inc(v.tensor_scalar(out=rowb[:, 0:TO], in0=s2row[:], scalar1=float(1.0 / 4096.0),
                                    scalar2=None, op0=Alu.mult))
            v.wait_ge(cDV.sem, cDV.n)
            cDV.inc(v.tensor_sub(out=rowb[:, 0:TO], in0=rowb[:, 0:TO], in1=rowa[:, 0:TO]))
            v.wait_ge(cDV.sem, cDV.n)
            cDV.inc(v.tensor_scalar(out=rowb[0:1, 0:TO], in0=rowb[0:1, 0:TO], scalar1=float(EPS),
                                    scalar2=None, op0=Alu.add))
        vvn = cDV.n
        @blk.scalar
        def _(sc, need=vvn):
            sc.wait_ge(cDV.sem, need)
            cSE.inc(sc.activation(out=rowb[0:1, 0:TO], in_=rowb[0:1, 0:TO], func=Act.Sqrt))
        ssn = cSE.n
        @blk.vector
        def _(v, need=ssn):
            v.wait_ge(cSE.sem, need)
            cDV.inc(v.reciprocal(out=r2row[:], in_=rowb[0:1, 0:TO]))

    def p4_out():
        for tt2 in range(2):
            @blk.tensor
            def _(t, tt2=tt2, need=cDV.n, ses=cSE.n):
                t.wait_ge(cDV.sem, need)
                t.wait_ge(cSE.sem, ses)
                cMM.inc(t.matmul(PP[4][:], onesr1F[:], m2row[0:1, tt2 * 512:(tt2 + 1) * 512],
                                 start=True, stop=True))
                cMM.inc(t.matmul(PP[5][:], onesr1F[:], r2row[0:1, tt2 * 512:(tt2 + 1) * 512],
                                 start=True, stop=True))
            bcn = cMM.n
            for do in range(32):
                @blk.sync
                def _(s, do=do, tt2=tt2, dv=cDV.n, stp=sOUT.n, sx=sX1.n):
                    s.wait_ge(cDV.sem, dv)
                    s.wait_ge(sOUT.sem, stp)
                    s.wait_ge(sX1.sem, sx)
                    sRT.inc(s.dma_start(out=rtile[:], in_=R2[do * 128:(do + 1) * 128,
                                                           tt2 * 512:(tt2 + 1) * 512]), 16)
                rtn = sRT.n
                @blk.vector
                def _(v, need=rtn, bc=bcn):
                    v.wait_ge(sRT.sem, need)
                    v.wait_ge(cMM.sem, bc)
                    cDV.inc(v.tensor_sub(out=rtile[:], in0=rtile[:], in1=PP[4][:]))
                    v.wait_ge(cDV.sem, cDV.n)
                    cDV.inc(v.tensor_mul(out=rtile[:], in0=rtile[:], in1=PP[5][:]))
                dvn = cDV.n
                @blk.scalar
                def _(sc, do=do, tt2=tt2, need=dvn):
                    sc.wait_ge(cDV.sem, need)
                    sOUT.inc(sc.dma_start(out=OUT[do * 128:(do + 1) * 128, tt2 * 512:(tt2 + 1) * 512],
                                          in_=rtile[:]), 16)

    if "p2b" in stages:
        p2b_wo()
    if "p2c" in stages:
        p2c_ln1()
    if "p3" in stages:
        p3_ffn()
    if "p4" in stages:
        p4_out()

    if dbg:
        @blk.sync
        def _(s):
            for c in ALL_CTRS:
                if c.n:
                    s.wait_ge(c.sem, c.n)
            for name, dst in dbg_map.items():
                src = dram[name]
                rows = src.shape[0]
                step = max(1, (1 << 20) // int(np.prod(src.shape[1:]) * 4))
                r = 0
                while r < rows:
                    rr = min(step, rows - r)
                    cLD.inc(s.dma_start(out=dst[r:r + rr], in_=src[r:r + rr]), 16)
                    r += rr
            s.wait_ge(cLD.sem, cLD.n)

    @blk.sync
    def _(s):
        for c in ALL_CTRS:
            if c.n:
                s.wait_ge(c.sem, c.n)

    es.close()
    return nc, dram


def rope_tables_np():
    import jax
    with jax.default_device(jax.devices("cpu")[0]):
        import jax.numpy as jnp
        half = 64
        inv_freq = 1.0 / 10000.0 ** (jnp.arange(0, half, dtype=jnp.float32) / half)
        pos = jnp.arange(2048, dtype=jnp.float32)
        ang = pos[:, None] * inv_freq[None, :]
        cos = np.asarray(jnp.cos(ang))
        sin = np.asarray(jnp.sin(ang))
    cosfm = np.ascontiguousarray(np.concatenate([cos.T, cos.T], axis=0).astype(np.float32))
    sinfm = np.ascontiguousarray(np.concatenate([-sin.T, sin.T], axis=0).astype(np.float32))
    return cosfm, sinfm


def host_inputs(core, X, Wq, Wk, Wv, Wo, qweight, lut4, lut8, bias):
    b, half = core >> 1, core & 1
    cosfm, sinfm = rope_tables_np()
    own_cols = np.concatenate([np.arange((2 * j + half) * 128, (2 * j + half) * 128 + 128) for j in range(8)])
    XTc = np.ascontiguousarray(X[b].T.astype(np.float32))
    XTOc = np.ascontiguousarray(XTc[:, own_cols])
    tk = np.arange(128)[:, None]
    tq = np.arange(128)[None, :]
    if half == 0:
        m0 = np.where(tk <= tq, 0.0, -1e30)
        m1 = np.full((128, 128), -1e30)
    else:
        m0 = np.zeros((128, 128))
        m1 = np.where(tk <= tq, 0.0, -1e30)
    masks = np.ascontiguousarray(np.stack([m0, m1]).astype(np.float32))
    qws = np.ascontiguousarray(
        qweight[:, 512 * core:512 * (core + 1), :].reshape(8, 4, 128, 128).transpose(0, 2, 1, 3))
    lut4s = np.ascontiguousarray(lut4[512 * core:512 * (core + 1)].reshape(4, 128, 16).transpose(1, 0, 2))
    lut8s = np.ascontiguousarray(lut8[512 * core:512 * (core + 1)].reshape(4, 128, 256).transpose(1, 0, 2))
    biasr = np.ascontiguousarray(bias.reshape(32, 128).T.astype(np.float32))
    return {
        "XT": XTc, "XTO": XTOc,
        "WQ": np.ascontiguousarray(Wq.astype(np.float32)),
        "WK": np.ascontiguousarray(Wk.astype(np.float32)),
        "WV": np.ascontiguousarray(Wv.astype(np.float32)),
        "WO": np.ascontiguousarray(Wo.astype(np.float32)),
        "QWS": qws, "LUT4S": lut4s, "LUT8S": lut8s, "BIASR": biasr,
        "COSO": np.ascontiguousarray(cosfm[:, own_cols]), "SINO": np.ascontiguousarray(sinfm[:, own_cols]),
        "COSC": cosfm, "SINC": sinfm,
        "MASKB": masks.astype(ml_dtypes.bfloat16),
        "IDENT": np.eye(128, dtype=ml_dtypes.bfloat16),
    }



from concourse.bass_utils import run_bass_kernel_spmd

_BUILD_CACHE = {}

def kernel(L0_attn_X, L0_attn_Wq, L0_attn_Wk, L0_attn_Wv, L0_attn_Wo,
           L0_ffn_qweight, L0_ffn_lut4, L0_ffn_lut8, L0_ffn_bias):
    X = np.asarray(L0_attn_X)
    Wq = np.asarray(L0_attn_Wq); Wk = np.asarray(L0_attn_Wk)
    Wv = np.asarray(L0_attn_Wv); Wo = np.asarray(L0_attn_Wo)
    qw = np.asarray(L0_ffn_qweight)
    lut4 = np.asarray(L0_ffn_lut4); lut8 = np.asarray(L0_ffn_lut8)
    bias = np.asarray(L0_ffn_bias)

    nc, dram = build(stages=("p1", "dq", "p2", "p2b", "p2c", "p3", "p4"))
    in_maps = [host_inputs(c, X, Wq, Wk, Wv, Wo, qw, lut4, lut8, bias) for c in range(8)]
    res = run_bass_kernel_spmd(nc, in_maps, core_ids=list(range(8)))
    out = np.empty((4, 2048, 4096), dtype=np.float32)
    for c in range(8):
        b, half = c >> 1, c & 1
        oc = np.asarray(res.results[c]["OUT"])
        for j in range(8):
            out[b, (2 * j + half) * 128:(2 * j + half) * 128 + 128, :] = \
                oc[:, j * 128:(j + 1) * 128].T
    return out
